# version 01
# baseline (speedup 1.0000x reference)
import sys
from contextlib import ExitStack

import numpy as np

sys.path.insert(0, "/opt/trn_rl_repo")
import ml_dtypes
import concourse.bass as bass
import concourse.mybir as mybir
from concourse.bass_utils import run_bass_kernel_spmd

F32 = mybir.dt.float32
BF16 = mybir.dt.bfloat16
AF = mybir.ActivationFunctionType
ALU = mybir.AluOpType
AX = mybir.AxisListType
BF = ml_dtypes.bfloat16

P = 128
T = 2048
NSEQ = 2
D = 1024
NCH = D // P
NH = 16
HD = 64
NEG = -1.0e30



def _f32(x):
    return np.ascontiguousarray(x, dtype=np.float32)


def host_constants():
    c = {}
    c["ident_f32"] = np.eye(P, dtype=np.float32)
    q = np.arange(P)[:, None]
    k = np.arange(P)[None, :]
    c["mask_diag_add"] = _f32(np.where(k > q, NEG, 0.0))
    c["maskT_keep"] = _f32(np.arange(P)[None, :] >= np.arange(P)[:, None])
    return c


def prep_weights_A(inp):
    w = {}
    w["A_wqkvT"] = _f32(np.concatenate([_f32(inp["L0_attn_Wqkv_w"]).T,
                                        _f32(inp["L0_attn_Wqkv_b"])[None, :]], 0))
    w["A_woT"] = _f32(np.concatenate([_f32(inp["L0_attn_Wo_w"]).T,
                                      _f32(inp["L0_attn_Wo_b"])[None, :]], 0))
    w["A_ghost"] = _f32(inp["L0_attn_ghost"]).reshape(1, NH)
    return w



ENGINES = ("sync", "tensor", "vector", "scalar", "gpsimd")


class Em:

    def __init__(self, nc, ctx):
        self.nc = nc
        self.ctx = ctx
        self.q = {e: [] for e in ENGINES}
        self.sems = {}
        self.ct = {}

    def sem(self, name):
        if name not in self.sems:
            self.sems[name] = self.ctx.enter_context(self.nc.semaphore(name))
            self.ct[name] = 0
        return self.sems[name]

    def c(self, name):
        self.sem(name)
        return self.ct[name]

    def emit(self, engine, fn, waits=(), inc=None, inc_amt=1):
        wl = []
        for name, v in waits:
            self.sem(name)
            assert v <= self.ct[name], f"wait {name}>={v} but only {self.ct[name]} planned"
            if v > 0:
                wl.append((self.sems[name], v))
        ih = None
        if inc is not None:
            ih = self.sem(inc)
            self.ct[inc] += inc_amt

        def run(eng, wl=wl, fn=fn, ih=ih, amt=inc_amt):
            for h, v in wl:
                eng.wait_ge(h, v)
            ins = fn(eng)
            if ih is not None:
                assert ins is not None, "inc on fn returning None"
                ins.then_inc(ih, amt)
        self.q[engine].append(run)
        return self.ct[inc] if inc is not None else None

    def dma(self, queue, out, in_, inc, waits=()):
        w = tuple(list(waits) + [(inc, self.c(inc))])
        return self.emit(queue, lambda e, o=out, i=in_: e.dma_start(out=o, in_=i),
                         waits=w, inc=inc, inc_amt=16)

    def wait(self, engine, name, value):
        self.emit(engine, lambda e: None, waits=((name, value),))

    def flush(self, block):
        for e in ENGINES:
            fns = self.q[e]
            if not fns:
                continue
            getattr(block, e)(lambda eng, fns=fns: [f(eng) for f in fns])


class Arena:
    def __init__(self, nc, ctx, name, n_elems, dtype, align=64):
        self.t = ctx.enter_context(nc.sbuf_tensor(name, [P, n_elems], dtype))
        self.n = n_elems
        self.off = 0
        self.align = align

    def alloc(self, n):
        n = -(-n // self.align) * self.align
        assert self.off + n <= self.n, f"arena overflow {self.off}+{n}>{self.n}"
        v = self.t[:, self.off:self.off + n]
        self.off += n
        return v




def emit_consts(E, A):
    ab, af = A["ab"], A["af"]
    C = {}
    C["ident"] = af.alloc(P)
    C["mdiag"] = af.alloc(P)
    C["maskT"] = ab.alloc(P)
    C["ones_col"] = ab.alloc(64)[:, 0:1]
    C["ones_row1"] = ab.alloc(P)[0:1, 0:P]
    C["ones_r512"] = ab.alloc(512)[0:1, :]
    C["ones_1x64"] = ab.alloc(64)[0:1, :]
    mstage = af.alloc(P)
    E.dma("sync", C["ident"], A["ident_f32"][:, :], inc="d_in")
    E.dma("sync", C["mdiag"], A["mask_diag_add"][:, :], inc="d_in")
    E.dma("sync", mstage, A["maskT_keep"][:, :], inc="d_in")
    d0 = E.ct["d_in"]
    E.emit("gpsimd", lambda e, o=C["maskT"], i=mstage: e.tensor_copy(out=o, in_=i),
           waits=(("d_in", d0),), inc="s_pool")
    for nm in ("ones_col", "ones_row1", "ones_r512", "ones_1x64"):
        E.emit("gpsimd", lambda e, v=C[nm]: e.memset(v, 1.0), inc="s_pool")
    C["ready_pool"] = E.ct["s_pool"]
    C["ready_din"] = d0
    return C


def emit_load_x_tokmajor(E, C, x_hbm_seq, x_f, stage_x, ps, extra_act_wait=()):
    for tc in range(16):
        stg = stage_x[:, 0:1024] if tc % 2 == 0 else stage_x[:, 1024:2048]
        w = [("s_pe", E.c("s_pe"))] if tc >= 2 else []
        E.dma("sync", stg, x_hbm_seq[128 * tc:128 * (tc + 1), :], inc="d_in", waits=tuple(w))
        din = E.ct["d_in"]
        for c in range(NCH):
            bank = ps[c % 2]
            waits = [("d_in", din)]
            if E.c("s_act") >= 1 and (tc > 0 or c >= 2):
                waits.append(("s_act", E.ct["s_act"] - 1))
            E.emit("tensor", lambda e, i=stg, c=c, b=bank, idn=C["ident"]: e.transpose(
                out=b[:, 0:P], in_=i[:, P * c:P * (c + 1)], identity=idn),
                waits=tuple(waits), inc="s_pe")
            aw = tuple([("s_pe", E.ct["s_pe"])] + list(extra_act_wait))
            E.emit("scalar", lambda e, b=bank, o=x_f[c][:, P * tc:P * (tc + 1)]: e.copy(
                out=o, in_=b[:, 0:P]),
                waits=aw, inc="s_act")
    return E.ct["s_act"]


class WLoader:

    def __init__(self, E, ab, af, wcols, bcols, tag):
        self.E = E
        self.stage = [af.alloc(wcols) for _ in range(2)]
        self.buf = [ab.alloc(wcols) for _ in range(2)]
        self.bstage = [af.alloc(bcols)[0:1, :] for _ in range(2)]
        self.bbuf = [ab.alloc(bcols)[0:1, :] for _ in range(2)]
        self.tag = tag
        self.k = 0
        self.pe_users = [0, 0]
        self.cast_ct = [0, 0]

    def load(self, hbm_main, hbm_bias, dc, cols):
        E = self.E
        par = self.k % 2
        self.k += 1
        dnm, cnm = "d_w" + self.tag, "c_w" + self.tag
        waits = [("s_pe", self.pe_users[par]), (cnm, self.cast_ct[par])]
        src = hbm_main.rearrange("(a p) c -> p a c", p=P)
        dst = self.stage[par][:, 0:dc * cols].rearrange("p (a c) -> p a c", a=dc)
        E.dma("sync", dst, src, inc=dnm, waits=tuple(waits))
        if hbm_bias is not None:
            E.dma("sync", self.bstage[par][0:1, 0:cols], hbm_bias, inc=dnm)
        dv = E.ct[dnm]
        E.emit("gpsimd", lambda e, s=self.stage[par], b=self.buf[par], n=dc * cols:
               e.tensor_copy(out=b[:, 0:n], in_=s[:, 0:n]),
               waits=((dnm, dv),), inc=cnm)
        if hbm_bias is not None:
            E.emit("gpsimd", lambda e, s=self.bstage[par], b=self.bbuf[par], n=cols:
                   e.tensor_copy(out=b[0:1, 0:n], in_=s[0:1, 0:n]),
                   waits=(), inc=cnm)
        self.cast_ct[par] = E.ct[cnm]
        bufv = self.buf[par][:, 0:dc * cols].rearrange("p (a c) -> p a c", a=dc)
        bv = self.bbuf[par][0:1, 0:cols] if hbm_bias is not None else None
        return bufv, bv, cnm, E.ct[cnm], par

    def done(self, par):
        self.pe_users[par] = self.E.c("s_pe")


def emit_proj_fmajor(E, C, wl, wT_hbm, ocs, src_chunks, dst_of_oc, ps, banks,
                     src_ready, dst_engine="vector", dst_fn=None, n_tt=4,
                     has_bias=True):
    nk = len(src_chunks)
    evac_ct = []
    for ii, oc in enumerate(ocs):
        colr = slice(P * oc, P * (oc + 1))
        wv, bv, cnm, cv, par = wl.load(
            wT_hbm[0:nk * P, colr],
            wT_hbm[nk * P:nk * P + 1, colr] if has_bias else None, nk, P)
        for tt in range(n_tt):
            step = ii * n_tt + tt
            bank = ps[banks[step % len(banks)]]
            bw = [evac_ct[step - len(banks)]] if step >= len(banks) else []
            for dc in range(nk):
                if dc == 0:
                    w = tuple(list(src_ready) + [(cnm, cv)] + bw)
                else:
                    w = ()
                lastnb = (not has_bias) and dc == nk - 1
                E.emit("tensor", lambda e, b=bank, wv=wv, dc=dc, s=src_chunks[dc], tt=tt,
                       sp=lastnb: e.matmul(out=b[:, :], lhsT=wv[:, dc, :],
                                           rhs=s[:, 512 * tt:512 * (tt + 1)],
                                           start=(dc == 0), stop=sp),
                       waits=w, inc="s_pe" if lastnb else None)
            if has_bias:
                E.emit("tensor", lambda e, b=bank, bv=bv, o1=C["ones_r512"]: e.matmul(
                    out=b[:, :], lhsT=bv, rhs=o1, start=False, stop=True),
                    waits=(("s_pool", C["ready_pool"]),), inc="s_pe")
            if dst_fn is not None:
                ec = dst_fn(oc, tt, bank, E.ct["s_pe"])
            else:
                dst = dst_of_oc(oc)
                ec = ("s_dve", E.emit(
                    dst_engine, lambda e, b=bank, o=dst[:, 512 * tt:512 * (tt + 1)]:
                    e.tensor_copy(out=o, in_=b[:, :]),
                    waits=(("s_pe", E.ct["s_pe"]),), inc="s_dve"))
            evac_ct.append(ec)
        wl.done(par)


def emit_ln_tile(E, C, sum_bf, x_out, sq_scr, ps_stat, ps_bc0, ps_bc1, sc,
                 sum_ready=(), eps=1e-5, gains=None, extra_out=None):
    svm = ps_stat[0:1, :]
    svq = ps_stat[32:33, :]
    for c in range(NCH):
        E.emit("tensor", lambda e, r=sum_bf[c], o=svm, st=(c == 0), sp=(c == NCH - 1),
               l=C["ones_col"]: e.matmul(out=o, lhsT=l, rhs=r, start=st, stop=sp),
               waits=sum_ready if c == 0 else (), inc="s_pe" if c == NCH - 1 else None)
    pe_mean = E.ct["s_pe"]
    for c in range(NCH):
        w = list(sum_ready) if c == 0 else [("s_pe", E.c("s_pe"))]
        E.emit("scalar", lambda e, i=sum_bf[c], o=sq_scr: e.activation(
            out=o, in_=i, func=AF.Square), waits=tuple(w), inc="s_act")
        E.emit("tensor", lambda e, o=svq, st=(c == 0), sp=(c == NCH - 1), r=sq_scr,
               l=C["ones_col"]: e.matmul(out=o, lhsT=l, rhs=r, start=st, stop=sp),
               waits=(("s_act", E.ct["s_act"]),), inc="s_pe")
    pe_sq = E.ct["s_pe"]
    E.emit("scalar", lambda e, o=sc["mean2"], i=svm: e.activation(
        out=o, in_=i, func=AF.Square, scale=1.0 / 1024.0),
        waits=(("s_pe", pe_mean),), inc="s_act")
    E.emit("scalar", lambda e, o=sc["mean_b"], i=svm: e.activation(
        out=o, in_=i, func=AF.Copy, scale=1.0 / 1024.0), inc="s_act")
    a_mean = E.ct["s_act"]
    E.emit("vector", lambda e, o=sc["var"], i=svq, m2=sc["mean2"]: e.scalar_tensor_tensor(
        out=o, in0=i, scalar=1.0 / 1024.0, in1=m2, op0=ALU.mult, op1=ALU.subtract),
        waits=(("s_act", a_mean), ("s_pe", pe_sq)), inc="s_dve")
    E.emit("vector", lambda e, v=sc["var"], ep=float(eps): e.tensor_scalar_add(
        out=v, in0=v, scalar1=ep), waits=(("s_dve", E.ct["s_dve"]),), inc="s_dve")
    E.emit("scalar", lambda e, v=sc["var"]: e.activation(out=v, in_=v, func=AF.Ln),
           waits=(("s_dve", E.ct["s_dve"]),), inc="s_act")
    E.emit("scalar", lambda e, o=sc["rstd_b"], v=sc["var"]: e.activation(
        out=o, in_=v, func=AF.Exp, scale=-0.5),
        waits=(("s_act", E.ct["s_act"]),), inc="s_act")
    a_rstd = E.ct["s_act"]
    E.emit("tensor", lambda e, o=ps_bc0, r=sc["mean_b"], l=C["ones_row1"]: e.matmul(
        out=o, lhsT=l, rhs=r, start=True, stop=True),
        waits=(("s_act", a_mean),), inc="s_pe")
    E.emit("tensor", lambda e, o=ps_bc1, r=sc["rstd_b"], l=C["ones_row1"]: e.matmul(
        out=o, lhsT=l, rhs=r, start=True, stop=True),
        waits=(("s_act", a_rstd),), inc="s_pe")
    pe_bc = E.ct["s_pe"]
    for c in range(NCH):
        E.emit("vector", lambda e, i=sum_bf[c], b=ps_bc0, o=sq_scr: e.tensor_tensor(
            out=o, in0=i, in1=b, op=ALU.subtract),
            waits=(("s_pe", pe_bc),) if c == 0 else (("s_dve", E.ct["s_dve"]),),
            inc="s_dve")
        if gains is None:
            E.emit("vector", lambda e, o=x_out[c], b=ps_bc1, i=sq_scr: e.tensor_tensor(
                out=o, in0=i, in1=b, op=ALU.mult),
                waits=(("s_dve", E.ct["s_dve"]),), inc="s_dve")
        else:
            g_col, b_col = gains
            E.emit("vector", lambda e, b=ps_bc1, i=sq_scr: e.tensor_tensor(
                out=i, in0=i, in1=b, op=ALU.mult),
                waits=(("s_dve", E.ct["s_dve"]),), inc="s_dve")
            E.emit("vector", lambda e, o=x_out[c], i=sq_scr, g=g_col[c], bb=b_col[c]:
                   e.tensor_scalar(out=o, in0=i, scalar1=g, scalar2=bb,
                                   op0=ALU.mult, op1=ALU.add),
                   waits=(("s_dve", E.ct["s_dve"]),), inc="s_dve")
        if extra_out is not None:
            extra_out(c)
    return E.ct["s_dve"]




def build_A(race_detect=False):
    nc = bass.Bass("TRN2", target_bir_lowering=False, detect_race_conditions=race_detect)
    x0 = nc.dram_tensor("x0", [NSEQ, T, D], F32, kind="ExternalInput").ap()
    wqkvT = nc.dram_tensor("A_wqkvT", [D + 1, 3 * D], F32, kind="ExternalInput").ap()
    woT = nc.dram_tensor("A_woT", [D + 1, D], F32, kind="ExternalInput").ap()
    ghost = nc.dram_tensor("A_ghost", [1, NH], F32, kind="ExternalInput").ap()
    identD = nc.dram_tensor("ident_f32", [P, P], F32, kind="ExternalInput").ap()
    mdiagD = nc.dram_tensor("mask_diag_add", [P, P], F32, kind="ExternalInput").ap()
    maskTD = nc.dram_tensor("maskT_keep", [P, P], F32, kind="ExternalInput").ap()
    xh_out = nc.dram_tensor("xh", [NSEQ, NCH, P, T], BF16, kind="ExternalOutput").ap()

    with ExitStack() as ctx:
        E = Em(nc, ctx)
        ab = Arena(nc, ctx, "ab", 79872, BF16)
        af = Arena(nc, ctx, "af", 8704, F32)
        A = {"ab": ab, "af": af, "ident_f32": identD, "mask_diag_add": mdiagD,
             "maskT_keep": maskTD}
        C = emit_consts(E, A)

        x_f = [ab.alloc(T) for _ in range(NCH)]
        Kf = [ab.alloc(T) for _ in range(NCH)]
        Af = [ab.alloc(T) for _ in range(NCH)]
        q_cur = ab.alloc(T)
        Vtok = ab.alloc(16 * 1040)
        qaug = ab.alloc(T)[0:HD + 1, :]
        kaug = ab.alloc(T)[0:HD + 1, :]
        pT = ab.alloc(T)
        rec_b = ab.alloc(512)[0:1, :]
        recB_sb = ab.alloc(512)
        sq_scr = ab.alloc(512)
        mrow_bf = ab.alloc(P)[0:16, :]
        sc = {"mean_b": ab.alloc(512)[0:1, :], "rstd_b": ab.alloc(512)[0:1, :]}

        ghost_sb = af.alloc(64)[0:1, 0:NH]
        stage_x = af.alloc(2048)
        mq = af.alloc(16)[:, 0:16]
        mq_tmp = af.alloc(64)[:, 0:1]
        rec_f = af.alloc(512)[0:1, :]
        sc["mean2"] = af.alloc(512)[0:1, :]
        sc["var"] = af.alloc(512)[0:1, :]

        wl = WLoader(E, ab, af, 1024, P, "w")

        ps = [ctx.enter_context(nc.psum_tensor(f"ps{i}", [P, 512], F32)) for i in range(8)]

        E.dma("sync", ghost_sb, ghost[:, :], inc="d_in")
        E.emit("gpsimd", lambda e, v=Vtok: e.memset(
            v.rearrange("p (a h c) -> p a h c", a=16, h=NH)[:, :, :, HD:HD + 1], 1.0),
            inc="s_pool")
        E.emit("gpsimd", lambda e, v=kaug[HD:HD + 1, :]: e.memset(v, 1.0), inc="s_pool")

        for s in range(NSEQ):
            E.wait("tensor", "s_dve", E.c("s_dve"))
            xw = ((("d_out", E.c("d_out")), ("s_dve", E.c("s_dve"))) if s > 0 else ())
            emit_load_x_tokmajor(E, C, x0[s], x_f, stage_x, ps, extra_act_wait=xw)
            x_ready = (("s_act", E.ct["s_act"]),)

            emit_proj_fmajor(E, C, wl, wqkvT[:, D:2 * D], list(range(NCH)), x_f,
                             lambda oc: Kf[oc], ps, [0, 1, 2, 3], x_ready)
            for oc in range(NCH):
                colr = slice(2 * D + P * oc, 2 * D + P * (oc + 1))
                wv, bv, cnm, cv, par = wl.load(wqkvT[0:D, colr], wqkvT[D:D + 1, colr],
                                               NCH, P)
                for tc in range(16):
                    step = oc * 16 + tc
                    bank = ps[4 + step % 2]
                    bw = [("s_dve", E.c("s_dve"))] if step >= 2 else []
                    for dc in range(NCH):
                        w = list(x_ready) + [(cnm, cv)] + bw
                        E.emit("tensor", lambda e, b=bank, wv=wv, dc=dc, tc=tc,
                               xs=x_f[dc]: e.matmul(
                                   out=b[:, 0:P], lhsT=xs[:, P * tc:P * (tc + 1)],
                                   rhs=wv[:, dc, :], start=(dc == 0), stop=False),
                               waits=tuple(w) if dc == 0 else ())
                    E.emit("tensor", lambda e, b=bank, bv=bv, o1=C["ones_row1"]: e.matmul(
                        out=b[:, 0:P], lhsT=o1, rhs=bv, start=False, stop=True),
                        inc="s_pe")
                    E.emit("vector", lambda e, b=bank, tc=tc, oc=oc, v=Vtok: e.tensor_copy(
                        out=v.rearrange("p (a h c) -> p a h c", a=16, h=NH)
                            [:, tc, 2 * oc:2 * oc + 2, 0:HD],
                        in_=b[:, 0:P].rearrange("p (h c) -> p h c", h=2)),
                        waits=(("s_pe", E.ct["s_pe"]),), inc="s_dve")
                wl.done(par)
            kv_done = (("s_dve", E.c("s_dve")),)

            for h in range(NH):
                ch, rb = h // 2, HD * (h % 2)
                if h % 2 == 0:
                    def qdst(oc, tt, bank, pev, q_cur=q_cur):
                        return ("s_dve", E.emit(
                            "vector", lambda e, b=bank, o=q_cur[:, 512 * tt:512 * (tt + 1)]:
                            e.tensor_copy(out=o, in_=b[:, :]),
                            waits=(("s_pe", pev),), inc="s_dve"))
                    emit_proj_fmajor(E, C, wl, wqkvT[:, 0:D], [ch], x_f, None, ps,
                                     [0, 1, 2, 3],
                                     tuple(list(x_ready) + [("s_dve", E.c("s_dve")),
                                                            ("s_act", E.c("s_act"))]),
                                     dst_fn=qdst)
                qrows = q_cur[rb:rb + HD, :]
                krows = Kf[ch][rb:rb + HD, :]
                q_ready = (("s_dve", E.c("s_dve")),)
                p1_step = 0
                p1_red_ct = []
                for j in range(16):
                    nk = P * (j + 1)
                    k0 = 0
                    first = True
                    while k0 < nk:
                        wdt = min(512, nk - k0)
                        bank = ps[p1_step % 2]
                        bw = ([("s_dve", p1_red_ct[p1_step - 2])] if p1_step >= 2
                              else [("s_act", E.c("s_act")), ("s_dve", E.c("s_dve"))])
                        E.emit("tensor", lambda e, b=bank, j=j, k0=k0, wdt=wdt, qr=qrows,
                               kr=krows: e.matmul(
                                   out=b[:, 0:wdt], lhsT=qr[:, P * j:P * (j + 1)],
                                   rhs=kr[:, k0:k0 + wdt], start=True, stop=True),
                               waits=tuple(list(q_ready) + bw), inc="s_pe")
                        mmv = E.ct["s_pe"]
                        if k0 + wdt == nk:
                            E.emit("vector", lambda e, b=bank, wdt=wdt, m=C["mdiag"]:
                                   e.tensor_tensor(out=b[:, wdt - P:wdt],
                                                   in0=b[:, wdt - P:wdt], in1=m, op=ALU.add),
                                   waits=(("s_pe", mmv), ("d_in", C["ready_din"])),
                                   inc="s_dve")
                        if first:
                            E.emit("vector", lambda e, b=bank, j=j, wdt=wdt, m=mq:
                                   e.reduce_max(out=m[:, j:j + 1], in_=b[:, 0:wdt], axis=AX.X),
                                   waits=(("s_pe", mmv), ("s_dve", E.c("s_dve"))),
                                   inc="s_dve")
                        else:
                            E.emit("vector", lambda e, b=bank, wdt=wdt, m=mq_tmp:
                                   e.reduce_max(out=m, in_=b[:, 0:wdt], axis=AX.X),
                                   waits=(("s_pe", mmv), ("s_dve", E.c("s_dve"))),
                                   inc="s_dve")
                            E.emit("vector", lambda e, j=j, m=mq, t=mq_tmp: e.tensor_tensor(
                                out=m[:, j:j + 1], in0=m[:, j:j + 1], in1=t, op=ALU.max),
                                waits=(("s_dve", E.ct["s_dve"]),), inc="s_dve")
                        p1_red_ct.append(E.ct["s_dve"])
                        first = False
                        k0 += wdt
                        p1_step += 1
                E.emit("tensor", lambda e, m=mq, idn=C["ident"], o=ps[6][0:16, 0:P]:
                       e.transpose(out=o, in_=m, identity=idn),
                       waits=(("s_dve", E.c("s_dve")),), inc="s_pe")
                E.emit("scalar", lambda e, o=mrow_bf, i=ps[6][0:16, 0:P]: e.mul(
                    out=o, in_=i, mul=-1.0),
                    waits=(("s_pe", E.ct["s_pe"]),), inc="s_act")
                E.dma("gpsimd", qaug[HD:HD + 1, :].rearrange("p (a b) -> p a b", a=16),
                      mrow_bf, inc="d_m",
                      waits=(("s_act", E.ct["s_act"]), ("s_pe", E.c("s_pe"))))
                E.emit("vector", lambda e, o=qaug[0:HD, :], i=qrows: e.tensor_copy(
                    out=o, in_=i),
                    waits=(("s_pe", E.c("s_pe")),), inc="s_dve")
                E.emit("vector", lambda e, o=kaug[0:HD, :], i=krows: e.tensor_copy(
                    out=o, in_=i), inc="s_dve")
                aug_done = (("s_dve", E.ct["s_dve"]), ("d_m", E.c("d_m")),
                            ("s_pool", C["ready_pool"]))

                steps = [(i, J) for i in range(16) for J in range(i // 4, 4)]
                exp_ct_of_step = {}
                for si, (i, J) in enumerate(steps):
                    q0 = P * i
                    J0 = i // 4
                    qs_ = max(q0, 512 * J)
                    wdt = 512 * (J + 1) - qs_
                    bank = ps[si % 2]
                    bw = [("s_act", exp_ct_of_step[si - 2])] if si >= 2 else \
                         [("s_act", E.c("s_act"))]
                    E.emit("tensor", lambda e, b=bank, i=i, qs_=qs_, wdt=wdt, ka=kaug,
                           qa=qaug: e.matmul(
                               out=b[:, 0:wdt], lhsT=ka[:, P * i:P * (i + 1)],
                               rhs=qa[:, qs_:qs_ + wdt], start=True, stop=True),
                           waits=tuple(list(aug_done) + bw) if J == J0 else tuple(bw),
                           inc="s_pe")
                    E.emit("scalar", lambda e, b=bank, qs_=qs_, wdt=wdt, p=pT: e.activation(
                        out=p[:, qs_:qs_ + wdt], in_=b[:, 0:wdt], func=AF.Exp, scale=0.125),
                        waits=(("s_pe", E.ct["s_pe"]),), inc="s_act")
                    exp_ct_of_step[si] = E.ct["s_act"]
                    if J == 3:
                        exp_done = E.ct["s_act"]
                        E.emit("vector", lambda e, q0=q0, p=pT, m=C["maskT"]: e.tensor_tensor(
                            out=p[:, q0:q0 + P], in0=p[:, q0:q0 + P], in1=m, op=ALU.mult),
                            waits=(("s_act", exp_done),), inc="s_dve")
                        msk_done = E.ct["s_dve"]
                        for J2 in range(J0, 4):
                            qs2 = max(q0, 512 * J2)
                            wdt2 = 512 * (J2 + 1) - qs2
                            off = qs2 - 512 * J2
                            E.emit("tensor", lambda e, i=i, J2=J2, qs2=qs2, wdt2=wdt2,
                                   off=off, h=h, v=Vtok, p=pT: e.matmul(
                                       out=ps[2 + J2][0:HD + 1, off:off + wdt2],
                                       lhsT=v[:, 1040 * i + 65 * h:1040 * i + 65 * (h + 1)],
                                       rhs=p[:, qs2:qs2 + wdt2],
                                       start=(i == 0), stop=(i == 4 * J2 + 3)),
                                   waits=(("s_dve", msk_done),) if J2 == J0 else (),
                                   inc="s_pe" if i == 4 * J2 + 3 else None)
                            if i == 4 * J2 + 3:
                                pvv = E.ct["s_pe"]
                                E.emit("vector", lambda e, J2=J2, h=h, o=rec_f, g=ghost_sb:
                                       e.tensor_scalar(out=o, in0=ps[2 + J2][HD:HD + 1, :],
                                                       scalar1=g[0:1, h:h + 1], scalar2=None,
                                                       op0=ALU.add),
                                       waits=(("s_pe", pvv), ("s_act", E.c("s_act"))),
                                       inc="s_dve")
                                E.emit("vector", lambda e, o=rec_f: e.reciprocal(
                                    out=o, in_=o),
                                    waits=(("s_dve", E.ct["s_dve"]),), inc="s_dve")
                                E.emit("vector", lambda e, o=rec_b, i2=rec_f: e.tensor_copy(
                                    out=o, in_=i2),
                                    waits=(("s_dve", E.ct["s_dve"]),), inc="s_dve")
                                E.emit("tensor", lambda e, o=ps[7][0:HD, :], r=rec_b,
                                       l=C["ones_1x64"]: e.matmul(
                                           out=o, lhsT=l, rhs=r, start=True, stop=True),
                                       waits=(("s_dve", E.ct["s_dve"]),), inc="s_pe")
                                E.emit("scalar", lambda e, o=recB_sb[0:HD, :],
                                       i2=ps[7][0:HD, :]: e.copy(out=o, in_=i2),
                                       waits=(("s_pe", E.ct["s_pe"]),
                                              ("s_dve", E.c("s_dve"))), inc="s_act")
                                E.emit("vector", lambda e, J2=J2, ch=ch, rb=rb,
                                       o=Af[ch], rB=recB_sb: e.tensor_tensor(
                                           out=o[rb:rb + HD, 512 * J2:512 * (J2 + 1)],
                                           in0=ps[2 + J2][0:HD, :], in1=rB[0:HD, :],
                                           op=ALU.mult),
                                       waits=(("s_act", E.ct["s_act"]),), inc="s_dve")

            attn_done = (("s_dve", E.c("s_dve")), ("s_act", E.c("s_act")))

            sum_full = Kf
            def wodst(oc, tt, bank, pev, sum_full=sum_full):
                return ("s_dve", E.emit(
                    "vector", lambda e, b=bank, o=sum_full[oc][:, 512 * tt:512 * (tt + 1)],
                    xr=x_f[oc][:, 512 * tt:512 * (tt + 1)]: e.tensor_tensor(
                        out=o, in0=b[:, :], in1=xr, op=ALU.add),
                    waits=(("s_pe", pev),), inc="s_dve"))
            emit_proj_fmajor(E, C, wl, woT, list(range(NCH)), Af, None, ps, [0, 1, 2, 3],
                             attn_done, dst_fn=wodst)
            wo_done = E.ct["s_dve"]
            for tt in range(4):
                sl = slice(512 * tt, 512 * (tt + 1))
                sum_t = [sum_full[c][:, sl] for c in range(NCH)]
                out_t = [x_f[c][:, sl] for c in range(NCH)]
                sr = (("s_dve", wo_done if tt == 0 else E.c("s_dve")),
                      ("s_act", E.c("s_act")))
                emit_ln_tile(E, C, sum_t, out_t, sq_scr, ps[4][:, :], ps[6][:, :], ps[7][:, :], sc,
                             sum_ready=sr)
            ln_done = E.c("s_dve")
            for c in range(NCH):
                E.dma("gpsimd", xh_out[s, c], x_f[c], inc="d_out",
                      waits=(("s_dve", ln_done),))
        E.wait("gpsimd", "d_out", E.c("d_out"))

        block = ctx.enter_context(nc.Block())
        E.flush(block)
    return nc




def prep_weights_B(inp):
    w = {}
    w["B_gateT"] = _f32(_f32(inp["L0_ffn_gate_w"]).T)
    w["B_cbias"] = _f32(inp["L0_ffn_correction_bias"]).reshape(1, 8)
    w["B_wg"] = _f32(np.concatenate([_f32(inp["L0_ffn_Wg"]).reshape(8 * D, 512),
                                     _f32(inp["L0_ffn_Wg_s"])], 0))
    w["B_wu"] = _f32(np.concatenate([_f32(inp["L0_ffn_Wu"]).reshape(8 * D, 512),
                                     _f32(inp["L0_ffn_Wu_s"])], 0))
    w["B_wd"] = _f32(np.concatenate([_f32(inp["L0_ffn_Wd"]).reshape(8 * 512, D),
                                     _f32(inp["L0_ffn_Wd_s"])], 0))
    return w


def build_B(race_detect=False, upto=4):
    nc = bass.Bass("TRN2", target_bir_lowering=False, detect_race_conditions=race_detect)
    xh_in = nc.dram_tensor("xh", [NSEQ, NCH, P, T], BF16, kind="ExternalInput").ap()
    gateT = nc.dram_tensor("B_gateT", [D, 8], F32, kind="ExternalInput").ap()
    cbias = nc.dram_tensor("B_cbias", [1, 8], F32, kind="ExternalInput").ap()
    wgD = nc.dram_tensor("B_wg", [9 * D, 512], F32, kind="ExternalInput").ap()
    wuD = nc.dram_tensor("B_wu", [9 * D, 512], F32, kind="ExternalInput").ap()
    wdD = nc.dram_tensor("B_wd", [9 * 512, D], F32, kind="ExternalInput").ap()
    identbD = nc.dram_tensor("ident_bf", [P, P], BF16, kind="ExternalInput").ap()
    xh_out = nc.dram_tensor("xh2", [NSEQ, NCH, P, T], BF16, kind="ExternalOutput").ap()

    with ExitStack() as ctx:
        E = Em(nc, ctx)
        ab = Arena(nc, ctx, "ab", 64000, BF16)
        af = Arena(nc, ctx, "af", 16000, F32)
        C = {}
        C["ones_col"] = ab.alloc(64)[:, 0:1]
        C["ones_row1"] = ab.alloc(P)[0:1, 0:P]
        C["ones_r512"] = ab.alloc(512)[0:1, :]
        C["identb"] = ab.alloc(P)
        onesrow_f = af.alloc(P)[0:1, 0:P]
        for nm in ("ones_col", "ones_row1", "ones_r512"):
            E.emit("gpsimd", lambda e, v=C[nm]: e.memset(v, 1.0), inc="s_pool")
        E.emit("gpsimd", lambda e, v=onesrow_f: e.memset(v, 1.0), inc="s_pool")
        C["ready_pool"] = E.ct["s_pool"]
        E.dma("sync", C["identb"], identbD[:, :], inc="d_in")

        x_f = [ab.alloc(T) for _ in range(NCH)]
        HF = [[ab.alloc(512) for _ in range(4)] for _ in range(9)]
        gate_bf = ab.alloc(64)
        silu_g = ab.alloc(512)
        htok = ab.alloc(512)
        sum_bf = [ab.alloc(512) for _ in range(NCH)]
        sq_scr = ab.alloc(512)
        sc = {"mean_b": ab.alloc(512)[0:1, :], "rstd_b": ab.alloc(512)[0:1, :]}

        CW = af.alloc(16 * 8)
        scores = af.alloc(64)[:, 0:8]
        scores_c = af.alloc(64)[:, 0:8]
        gs = af.alloc(64)[:, 0:4]
        cntg = af.alloc(64)[:, 0:4]
        tmp4 = af.alloc(64)[:, 0:4]
        vv8 = af.alloc(64)[:, 0:8]
        z8 = af.alloc(64)[:, 0:8]
        zs8 = af.alloc(64)[:, 0:8]
        cnt8 = af.alloc(64)[:, 0:8]
        tmp8 = af.alloc(64)[:, 0:8]
        ones8 = af.alloc(64)[:, 0:8]
        wsum = af.alloc(64)[:, 0:1]
        biasB = af.alloc(64)[:, 0:8]
        cb_sb = af.alloc(64)[0:1, 0:8]
        gate_stage = af.alloc(64)
        sc["mean2"] = af.alloc(512)[0:1, :]
        sc["var"] = af.alloc(512)[0:1, :]

        wl = WLoader(E, ab, af, 4096, 64, "w")
        wld = WLoader(E, ab, af, 2048, 64, "d")

        ps = [ctx.enter_context(nc.psum_tensor(f"ps{i}", [P, 512], F32)) for i in range(7)]
        psb = ctx.enter_context(nc.psum_tensor("psb", [P, 512], BF16))

        E.emit("gpsimd", lambda e, v=ones8: e.memset(v, 1.0), inc="s_pool")
        ones8_ok = E.ct["s_pool"]
        E.dma("sync", gate_stage[:, 0:64].rearrange("p (a c) -> p a c", a=8),
              gateT.rearrange("(a p) c -> p a c", p=P), inc="d_in")
        E.dma("sync", cb_sb, cbias[:, :], inc="d_in")
        din0 = E.ct["d_in"]
        E.emit("gpsimd", lambda e, o=gate_bf, i=gate_stage: e.tensor_copy(
            out=o[:, 0:64], in_=i[:, 0:64]), waits=(("d_in", din0),), inc="s_pool")
        gate_ready = (("s_pool", E.ct["s_pool"]),)
        E.emit("tensor", lambda e, o=ps[6][:, 0:8], l=onesrow_f, r=cb_sb: e.matmul(
            out=o, lhsT=l, rhs=r, start=True, stop=True),
            waits=(("d_in", din0), ("s_pool", C["ready_pool"])), inc="s_pe")
        E.emit("scalar", lambda e, o=biasB, i=ps[6][:, 0:8]: e.copy(out=o, in_=i),
               waits=(("s_pe", E.ct["s_pe"]),), inc="s_act")
        bias_ready = (("s_act", E.ct["s_act"]),)

        for s in range(NSEQ):
            xw = [("s_dve", E.c("s_dve")), ("d_out", E.c("d_out"))]
            for c in range(NCH):
                E.dma("sync", x_f[c], xh_in[s, c], inc="d_in", waits=tuple(xw))
            x_ready = (("d_in", E.ct["d_in"]),)

            for tc in range(16 if upto >= 1 else 0):
                bank = ps[5]
                for dc in range(NCH):
                    w = list(x_ready) + list(gate_ready) + \
                        [("s_act", E.c("s_act"))] if dc == 0 else ()
                    E.emit("tensor", lambda e, b=bank, dc=dc, tc=tc, g=gate_bf,
                           xs=x_f[dc]: e.matmul(
                               out=b[:, 0:8], lhsT=xs[:, P * tc:P * (tc + 1)],
                               rhs=g[:, 8 * dc:8 * (dc + 1)],
                               start=(dc == 0), stop=(dc == NCH - 1)),
                           waits=tuple(w) if dc == 0 else (),
                           inc="s_pe" if dc == NCH - 1 else None)
                mmv = E.ct["s_pe"]
                E.emit("scalar", lambda e, b=bank, o=scores: e.activation(
                    out=o, in_=b[:, 0:8], func=AF.Sigmoid),
                    waits=(("s_pe", mmv), ("s_dve", E.c("s_dve"))), inc="s_act")
                dv = lambda: ("s_dve", E.c("s_dve"))
                av = ("s_act", E.ct["s_act"])
                E.emit("vector", lambda e, o=scores_c, i=scores, b=biasB: e.tensor_tensor(
                    out=o, in0=i, in1=b, op=ALU.add),
                    waits=(av,) + tuple(list(bias_ready)), inc="s_dve")
                E.emit("vector", lambda e, o=gs, i=scores_c: e.tensor_tensor(
                    out=o, in0=i[:, 0:8:2], in1=i[:, 1:8:2], op=ALU.add),
                    waits=(dv(),), inc="s_dve")
                for g in range(4):
                    E.emit("vector", lambda e, o=(cntg if g == 0 else tmp4), i=gs, g=g:
                           e.tensor_scalar(out=o, in0=i, scalar1=i[:, g:g + 1],
                                           scalar2=None, op0=ALU.is_lt),
                           waits=(dv(),), inc="s_dve")
                    if g > 0:
                        E.emit("vector", lambda e, o=cntg, t=tmp4: e.tensor_tensor(
                            out=o, in0=o, in1=t, op=ALU.add),
                            waits=(dv(),), inc="s_dve")
                E.emit("vector", lambda e, o=cntg: e.tensor_scalar(
                    out=o, in0=o, scalar1=2.0, scalar2=None, op0=ALU.is_lt),
                    waits=(dv(),), inc="s_dve")
                for par in range(2):
                    E.emit("vector", lambda e, o=vv8, i=scores_c, m=cntg, par=par:
                           e.tensor_tensor(out=o[:, par:8:2], in0=i[:, par:8:2], in1=m,
                                           op=ALU.mult),
                           waits=(dv(),), inc="s_dve")
                E.emit("vector", lambda e, o=z8, i=vv8: e.tensor_scalar(
                    out=o, in0=i, scalar1=0.0, scalar2=None, op0=ALU.is_equal),
                    waits=(dv(),), inc="s_dve")
                E.emit("vector", lambda e, o=zs8, a=ones8, b=z8: e.tensor_tensor_scan(
                    out=o, data0=a, data1=b, initial=0.0, op0=ALU.mult, op1=ALU.add),
                    waits=(dv(), ("s_pool", ones8_ok)), inc="s_dve")
                E.emit("vector", lambda e, o=zs8, b=z8: e.tensor_tensor(
                    out=o, in0=o, in1=b, op=ALU.subtract),
                    waits=(dv(),), inc="s_dve")
                for ee in range(8):
                    E.emit("vector", lambda e, o=(cnt8 if ee == 0 else tmp8), i=vv8, ee=ee:
                           e.tensor_scalar(out=o, in0=i, scalar1=i[:, ee:ee + 1],
                                           scalar2=None, op0=ALU.is_lt),
                           waits=(dv(),), inc="s_dve")
                    if ee > 0:
                        E.emit("vector", lambda e, o=cnt8, t=tmp8: e.tensor_tensor(
                            out=o, in0=o, in1=t, op=ALU.add),
                            waits=(dv(),), inc="s_dve")
                E.emit("vector", lambda e, o=tmp8, a=z8, b=zs8: e.tensor_tensor(
                    out=o, in0=a, in1=b, op=ALU.mult), waits=(dv(),), inc="s_dve")
                E.emit("vector", lambda e, o=cnt8, t=tmp8: e.tensor_tensor(
                    out=o, in0=o, in1=t, op=ALU.add), waits=(dv(),), inc="s_dve")
                E.emit("vector", lambda e, o=cnt8: e.tensor_scalar(
                    out=o, in0=o, scalar1=4.0, scalar2=None, op0=ALU.is_lt),
                    waits=(dv(),), inc="s_dve")
                E.emit("vector", lambda e, o=tmp8, m=cnt8, sco=scores: e.tensor_tensor(
                    out=o, in0=m, in1=sco, op=ALU.mult), waits=(dv(),), inc="s_dve")
                E.emit("vector", lambda e, o=wsum, i=tmp8: e.reduce_sum(
                    out=o, in_=i, axis=AX.X), waits=(dv(),), inc="s_dve")
                E.emit("vector", lambda e, o=wsum: e.tensor_scalar_add(
                    out=o, in0=o, scalar1=1e-20), waits=(dv(),), inc="s_dve")
                E.emit("vector", lambda e, o=wsum: e.reciprocal(out=o, in_=o),
                       waits=(dv(),), inc="s_dve")
                E.emit("vector", lambda e, o=CW[:, 8 * tc:8 * (tc + 1)], i=tmp8, r=wsum:
                       e.tensor_scalar(out=o, in0=i, scalar1=r, scalar2=2.5,
                                       op0=ALU.mult, op1=ALU.mult),
                       waits=(dv(),), inc="s_dve")
            rout_done = E.ct["s_dve"]

            for stt in range(4 if upto >= 2 else 0):
                for e9 in range(9):
                    wvg, _, cnmg, cvg, parg = wl.load(wgD[D * e9:D * (e9 + 1), :], None,
                                                      NCH, 512)
                    wvu, _, cnmu, cvu, paru = wl.load(wuD[D * e9:D * (e9 + 1), :], None,
                                                      NCH, 512)
                    for st in range(4):
                        tc = stt * 4 + st
                        t0 = P * tc
                        for (bank, wv, cnm, cv) in ((ps[0], wvg, cnmg, cvg),
                                                    (ps[1], wvu, cnmu, cvu)):
                            for dc in range(NCH):
                                w = (tuple(x_ready) + ((cnm, cv), ("s_act", E.c("s_act")),
                                     ("s_dve", E.c("s_dve")))) if dc == 0 else ()
                                E.emit("tensor", lambda e, b=bank, wv=wv, dc=dc, t0=t0,
                                       xs=x_f[dc]: e.matmul(
                                           out=b[:, :], lhsT=xs[:, t0:t0 + P],
                                           rhs=wv[:, dc, :], start=(dc == 0),
                                           stop=(dc == NCH - 1)),
                                       waits=w, inc="s_pe" if dc == NCH - 1 else None)
                        guv = E.ct["s_pe"]
                        E.emit("scalar", lambda e, o=silu_g, b=ps[0]: e.activation(
                            out=o, in_=b[:, :], func=AF.Silu),
                            waits=(("s_pe", guv),), inc="s_act")
                        hw = [("s_pe", guv), ("s_act", E.ct["s_act"]),
                              ("s_dve", rout_done)]
                        if e9 < 8:
                            E.emit("vector", lambda e, o=htok, sg=silu_g, b=ps[1],
                                   cwc=CW[:, 8 * tc + e9:8 * tc + e9 + 1]:
                                   e.scalar_tensor_tensor(out=o, in0=sg, scalar=cwc,
                                                          in1=b[:, :], op0=ALU.mult,
                                                          op1=ALU.mult),
                                   waits=tuple(hw), inc="s_dve")
                        else:
                            E.emit("vector", lambda e, o=htok, sg=silu_g, b=ps[1]:
                                   e.tensor_tensor(out=o, in0=sg, in1=b[:, :],
                                                   op=ALU.mult),
                                   waits=tuple(hw), inc="s_dve")
                        hdv = E.ct["s_dve"]
                        for ic in range(4):
                            E.emit("tensor", lambda e, o=psb[:, 0:P],
                                   i=htok, ic=ic, idn=C["identb"]: e.transpose(
                                       out=o, in_=i[:, P * ic:P * (ic + 1)], identity=idn),
                                   waits=(("s_dve", hdv), ("s_act", E.c("s_act")),
                                          ("d_in", E.c("d_in"))) if ic == 0 else
                                         (("s_act", E.c("s_act")),),
                                   inc="s_pe")
                            E.emit("scalar", lambda e, o=HF[e9][ic],
                                   i=psb[:, 0:P],
                                   st=st: e.copy(out=o[:, P * st:P * (st + 1)], in_=i),
                                   waits=(("s_pe", E.ct["s_pe"]),), inc="s_act")
                    wl.done(parg)
                    wl.done(paru)
                hf_done = (("s_act", E.c("s_act")),)

                for hh in range(2 if upto >= 3 else 0):
                    banks_done = []
                    for e9 in range(9):
                        wv, _, cnm, cv, par = wld.load(
                            wdD[512 * e9:512 * (e9 + 1), 512 * hh:512 * (hh + 1)],
                            None, 4, 512)
                        for hcr in range(4):
                            bank = ps[hcr % 4]
                            for ic in range(4):
                                first = (e9 == 0 and ic == 0)
                                w = ()
                                if first:
                                    w = (tuple(hf_done) + ((cnm, cv),
                                         ("s_dve", E.c("s_dve")),
                                         ("s_act", E.c("s_act"))))
                                elif ic == 0:
                                    w = ((cnm, cv),)
                                E.emit("tensor", lambda e, b=bank, wv=wv, ic=ic,
                                       hcr=hcr, hf=HF[e9][ic],
                                       fr=(e9 == 0 and ic == 0),
                                       lst=(e9 == 8 and ic == 3): e.matmul(
                                           out=b[:, :],
                                           lhsT=wv[:, ic, P * hcr:P * (hcr + 1)],
                                           rhs=hf[:, :],
                                           start=fr, stop=lst),
                                       waits=w,
                                       inc="s_pe" if (hcr == 3 and ic == 3) else None)
                        wld.done(par)
                    for hcr in range(4):
                        hc = 4 * hh + hcr
                        bank = ps[hcr % 4]
                        E.emit("vector", lambda e, b=bank, hc=hc, stt=stt,
                               o=sum_bf[hc], xr=x_f[hc]: e.tensor_tensor(
                                   out=o, in0=b[:, :],
                                   in1=xr[:, 512 * stt:512 * (stt + 1)],
                                   op=ALU.add),
                               waits=(("s_pe", E.c("s_pe")),), inc="s_dve")
                moe_done = E.ct["s_dve"]
                if upto >= 4:
                    slx = slice(512 * stt, 512 * (stt + 1))
                    sum_t = [sum_bf[c] for c in range(NCH)]
                    out_t = [x_f[c][:, slx] for c in range(NCH)]
                    sr = (("s_dve", moe_done), ("s_act", E.c("s_act")))
                    emit_ln_tile(E, C, sum_t, out_t, sq_scr, ps[4][:, :], ps[5][:, :],
                                 ps[6][:, :], sc, sum_ready=sr)
            ln_done = E.c("s_dve")
            for c in range(NCH):
                E.dma("gpsimd", xh_out[s, c], x_f[c], inc="d_out",
                      waits=(("s_dve", ln_done),))
        E.wait("gpsimd", "d_out", E.c("d_out"))

        block = ctx.enter_context(nc.Block())
        E.flush(block)
    return nc



DI = 2048
DS = 16
QW = 512


def prep_weights_C(inp):
    w = {}
    w["C_winT"] = _f32(_f32(inp["L1_attn_W_in"]).T)
    w["C_wxT"] = _f32(_f32(inp["L1_attn_W_x"]).T)
    w["C_wdtT"] = _f32(np.concatenate([_f32(inp["L1_attn_W_dt"]).T,
                                       _f32(inp["L1_attn_b_dt"])[None, :]], 0))
    w["C_woutT"] = _f32(_f32(inp["L1_attn_W_out"]).T)
    w["C_lng"] = _f32(_f32(inp["L1_attn_ln_g"]).reshape(NCH, P).T)
    w["C_lnb"] = _f32(_f32(inp["L1_attn_ln_b"]).reshape(NCH, P).T)
    w["C_convw"] = _f32(_f32(inp["L1_attn_conv_w"]).reshape(16, P, 4)
                        .transpose(1, 0, 2).reshape(P, 64))
    w["C_convb"] = _f32(_f32(inp["L1_attn_conv_b"]).reshape(16, P).T)
    w["C_alog"] = _f32(_f32(inp["L1_attn_A_log"]).reshape(16, P, DS)
                       .transpose(1, 0, 2).reshape(P, 256))
    w["C_dpar"] = _f32(_f32(inp["L1_attn_D_param"]).reshape(16, P).T)
    w["C_alpha"] = _f32(inp["L1_attn_alpha"]).reshape(1, 1)
    return w


def build_C(race_detect=False, upto=9):
    nc = bass.Bass("TRN2", target_bir_lowering=False, detect_race_conditions=race_detect)
    xh_in = nc.dram_tensor("xh", [NSEQ, NCH, P, T], BF16, kind="ExternalInput").ap()
    winT = nc.dram_tensor("C_winT", [D, 2 * DI], F32, kind="ExternalInput").ap()
    wxT = nc.dram_tensor("C_wxT", [DI, 96], F32, kind="ExternalInput").ap()
    wdtT = nc.dram_tensor("C_wdtT", [65, DI], F32, kind="ExternalInput").ap()
    woutT = nc.dram_tensor("C_woutT", [DI, D], F32, kind="ExternalInput").ap()
    lngD = nc.dram_tensor("C_lng", [P, NCH], F32, kind="ExternalInput").ap()
    lnbD = nc.dram_tensor("C_lnb", [P, NCH], F32, kind="ExternalInput").ap()
    convwD = nc.dram_tensor("C_convw", [P, 64], F32, kind="ExternalInput").ap()
    convbD = nc.dram_tensor("C_convb", [P, 16], F32, kind="ExternalInput").ap()
    alogD = nc.dram_tensor("C_alog", [P, 256], F32, kind="ExternalInput").ap()
    dparD = nc.dram_tensor("C_dpar", [P, 16], F32, kind="ExternalInput").ap()
    alphaD = nc.dram_tensor("C_alpha", [1, 1], F32, kind="ExternalInput").ap()
    identbD = nc.dram_tensor("ident_bf", [P, P], BF16, kind="ExternalInput").ap()
    xh_out = nc.dram_tensor("xh3", [NSEQ, NCH, P, T], BF16, kind="ExternalOutput").ap()
    sresS = nc.dram_tensor("C_sres", [NSEQ, 16, P, T], BF16, kind="Internal").ap()
    dtS = nc.dram_tensor("C_dt", [NSEQ, 16, P, T], BF16, kind="Internal").ap()

    with ExitStack() as ctx:
        E = Em(nc, ctx)
        ab = Arena(nc, ctx, "ab", 64640, BF16)
        af = Arena(nc, ctx, "af", 15616, F32)
        C = {}
        C["ones_col"] = ab.alloc(64)[:, 0:1]
        C["ones_row1"] = ab.alloc(P)[0:1, 0:P]
        C["ones_r512"] = ab.alloc(512)[0:1, :]
        C["identb"] = ab.alloc(P)
        brow_bf = ab.alloc(512)[0:1, :]
        for nm in ("ones_col", "ones_row1", "ones_r512"):
            E.emit("gpsimd", lambda e, v=C[nm]: e.memset(v, 1.0), inc="s_pool")
        C["ready_pool"] = E.ct["s_pool"]

        XH = [ab.alloc(T) for _ in range(16)]
        R2 = [ab.alloc(T) for _ in range(NCH)]
        Rxt = ab.alloc(4096)
        tmpx = ab.alloc(2176)
        dt_tile = ab.alloc(512)
        sres_tile = ab.alloc(512)
        ubuf = ab.alloc(512)
        abuf = ab.alloc(512)
        hbuf = ab.alloc(512)
        tmpy = ab.alloc(512)

        wdt_bf = R2[0]
        dtr_aug = R2[1]
        Bbc = [R2[n % 4][:, QW * (n // 4):QW * (n // 4 + 1)] for n in range(16)]
        Cbc = [R2[4 + n % 4][:, QW * (n // 4):QW * (n // 4 + 1)] for n in range(16)]
        sum_bf = [R2[i // 4][:, QW * (i % 4):QW * (i % 4 + 1)] for i in range(8)]
        sq_scr = ubuf
        sc = {"mean_b": abuf[0:1, 0:512], "rstd_b": tmpy[0:1, 0:512]}
        xt8 = [Rxt[:, 512 * i:512 * (i + 1)] for i in range(8)]
        bc_bf = Rxt[:, 0:2048]
        xres_t = Rxt[:, 2048:2560]
        sq5 = Rxt[:, 2560:3072]
        sc5 = {"mean_b": Rxt[0:1, 3072:3584], "rstd_b": Rxt[0:1, 3584:4096]}

        lng = af.alloc(NCH)[:, 0:NCH]
        lnb = af.alloc(NCH)[:, 0:NCH]
        convw = af.alloc(64)
        convb = af.alloc(16)[:, 0:16]
        A_sb = af.alloc(256)
        dpar = af.alloc(16)[:, 0:16]
        alpha_sb = af.alloc(64)[0:1, 0:1]
        alpha_col = af.alloc(64)[:, 0:1]
        onesrow_f = af.alloc(P)[0:1, 0:P]
        x_dbl = af.alloc(T)
        carry = af.alloc(256)
        tz = af.alloc(512)
        mx = af.alloc(512)
        yacc = af.alloc(512)
        convacc = af.alloc(2048)
        wdt_stage = af.alloc(2048)
        sc["mean2"] = af.alloc(512)[0:1, :]
        sc["var"] = af.alloc(512)[0:1, :]
        sc5["mean2"] = af.alloc(512)[0:1, :]
        sc5["var"] = af.alloc(512)[0:1, :]

        wl = WLoader(E, ab, af, 2048, P, "w")

        ps = [ctx.enter_context(nc.psum_tensor(f"ps{i}", [P, 512], F32)) for i in range(8)]

        E.emit("gpsimd", lambda e, v=onesrow_f: e.memset(v, 1.0), inc="s_pool")
        for (dst, srcD) in ((lng, lngD), (lnb, lnbD), (convw, convwD),
                            (convb, convbD), (dpar, dparD), (alpha_sb, alphaD)):
            E.dma("sync", dst, srcD[:, :], inc="d_c")
        E.dma("sync", A_sb, alogD[:, :], inc="d_c")
        dcv = E.ct["d_c"]
        E.emit("scalar", lambda e, v=A_sb: e.activation(out=v, in_=v, func=AF.Exp),
               waits=(("d_c", dcv),), inc="s_act")
        E.emit("vector", lambda e, v=A_sb: e.tensor_scalar_mul(out=v, in0=v, scalar1=-1.0),
               waits=(("s_act", E.ct["s_act"]),), inc="s_dve")
        E.emit("tensor", lambda e, o=ps[6][:, 0:1], l=onesrow_f, r=alpha_sb: e.matmul(
            out=o, lhsT=l, rhs=r, start=True, stop=True),
            waits=(("d_c", dcv), ("s_pool", E.c("s_pool"))), inc="s_pe")
        E.emit("scalar", lambda e, o=alpha_col, i=ps[6][:, 0:1]: e.copy(out=o, in_=i),
               waits=(("s_pe", E.ct["s_pe"]),), inc="s_act")
        E.emit("gpsimd", lambda e, o=tmpx: e.memset(o[:, 0:64], 0.0), inc="s_pool")
        E.dma("sync", C["identb"], identbD[:, :], inc="d_c")
        consts_ok = (("s_pool", E.ct["s_pool"]), ("s_act", E.c("s_act")),
                     ("s_dve", E.c("s_dve")))
        gcols = [lng[:, c:c + 1] for c in range(NCH)]
        bcols = [lnb[:, c:c + 1] for c in range(NCH)]

        for s in range(NSEQ):
            for tt in range(4):
                xw = [("s_dve", E.c("s_dve")), ("s_pe", E.c("s_pe")),
                      ("d_out", E.c("d_out")), ("s_act", E.c("s_act"))]
                for c in range(NCH):
                    E.dma("sync", xt8[c], xh_in[s, c, :, 512 * tt:512 * (tt + 1)],
                          inc="d_in", waits=tuple(xw) if c == 0 else ())
                sl = slice(512 * tt, 512 * (tt + 1))
                out_t = [R2[c][:, sl] for c in range(NCH)]
                sr = (("d_in", E.ct["d_in"]),) + consts_ok + \
                     (("s_dve", E.c("s_dve")), ("s_act", E.c("s_act")))
                emit_ln_tile(E, C, xt8, out_t, sq_scr, ps[4][:, :], ps[5][:, :],
                             ps[6][:, :], sc, sum_ready=sr, gains=(gcols, bcols))
            h_done = (("s_dve", E.c("s_dve")),)
            if upto < 1:
                continue

            for oc in range(16):
                def xdst(oc2, tt, bank, pev):
                    return ("s_dve", E.emit(
                        "vector",
                        lambda e, b=bank, o=tmpx[:, 64 + 512 * tt:64 + 512 * (tt + 1)]:
                        e.tensor_copy(out=o, in_=b[:, :]),
                        waits=(("s_pe", pev), ("s_act", E.c("s_act"))), inc="s_dve"))
                emit_proj_fmajor(E, C, wl, winT, [oc], R2, None, ps, [0, 1, 2, 3],
                                 h_done + (("s_dve", E.c("s_dve")), ("s_pool",
                                 C["ready_pool"])), dst_fn=xdst, has_bias=False)
                for j in range(4):
                    if j == 0:
                        E.emit("vector", lambda e, o=convacc, i=tmpx, w=convw, oc=oc:
                               e.tensor_scalar(out=o, in0=i[:, 61:61 + T],
                                               scalar1=w[:, 4 * oc:4 * oc + 1],
                                               scalar2=None, op0=ALU.mult),
                               waits=(("s_dve", E.c("s_dve")), ("s_act", E.c("s_act"))),
                               inc="s_dve")
                    else:
                        E.emit("vector", lambda e, o=convacc, i=tmpx, w=convw, oc=oc, j=j:
                               e.scalar_tensor_tensor(out=o, in0=i[:, 61 + j:61 + j + T],
                                                      scalar=w[:, 4 * oc + j:4 * oc + j + 1],
                                                      in1=o, op0=ALU.mult, op1=ALU.add),
                               waits=(("s_dve", E.ct["s_dve"]),), inc="s_dve")
                E.emit("scalar", lambda e, o=XH[oc], i=convacc, b=convb, oc=oc:
                       e.activation(out=o, in_=i, func=AF.Silu, bias=b[:, oc:oc + 1]),
                       waits=(("s_dve", E.ct["s_dve"]),), inc="s_act")
            for oc in range(16):
                def rdst(oc2, tt, bank, pev, oc=oc):
                    E.emit("scalar", lambda e, b=bank, o=sres_tile: e.activation(
                        out=o, in_=b[:, :], func=AF.Silu),
                        waits=(("s_pe", pev), ("d_sr", E.c("d_sr"))), inc="s_act")
                    ct = E.dma("gpsimd", sresS[s, oc, :, 512 * tt:512 * (tt + 1)],
                               sres_tile, inc="d_sr",
                               waits=(("s_act", E.ct["s_act"]),))
                    return ("d_sr", ct)
                emit_proj_fmajor(E, C, wl, winT[:, DI:2 * DI], [oc], R2, None, ps,
                                 [0, 1, 2, 3],
                                 h_done + (("s_dve", E.c("s_dve")),
                                           ("s_act", E.c("s_act"))),
                                 dst_fn=rdst, has_bias=False)
            xh_done = (("s_act", E.c("s_act")),)
            if upto < 2:
                continue

            wv16, _, cnm16, cv16, par16 = wl.load(wxT[:, :], None, 16, 96)
            for tt in range(4):
                bank = ps[4 + tt % 2]
                for dc in range(16):
                    w = (tuple(xh_done) + ((cnm16, cv16), ("s_dve", E.c("s_dve")),
                         ("s_act", E.c("s_act")))) if dc == 0 else ()
                    E.emit("tensor", lambda e, b=bank, wv=wv16, dc=dc, tt=tt,
                           xs=XH[dc]: e.matmul(
                               out=b[0:96, :], lhsT=wv[:, dc, 0:96],
                               rhs=xs[:, 512 * tt:512 * (tt + 1)],
                               start=(dc == 0), stop=(dc == 15)),
                           waits=w, inc="s_pe" if dc == 15 else None)
                E.emit("vector", lambda e, b=bank, o=x_dbl, tt=tt: e.tensor_copy(
                    out=o[0:96, 512 * tt:512 * (tt + 1)], in_=b[0:96, :]),
                    waits=(("s_pe", E.ct["s_pe"]),), inc="s_dve")
            wl.done(par16)
            E.dma("sync", wdt_stage[0:65, :], wdtT[:, :], inc="d_c",
                  waits=(("s_pool", E.c("s_pool")),))
            E.emit("gpsimd", lambda e, o=wdt_bf, i=wdt_stage: e.tensor_copy(
                out=o[0:65, :], in_=i[0:65, :]),
                waits=(("d_c", E.ct["d_c"]), ("s_pe", E.c("s_pe"))), inc="s_pool")
            E.emit("gpsimd", lambda e, o=dtr_aug: e.memset(o[64:65, :], 1.0),
                   waits=(("s_pe", E.c("s_pe")),), inc="s_pool")
            E.emit("vector", lambda e, o=dtr_aug, i=x_dbl: e.tensor_copy(
                out=o[0:64, :], in_=i[0:64, :]),
                waits=(("s_dve", E.ct["s_dve"]), ("s_pool", E.ct["s_pool"])), inc="s_dve")
            dtr_ok = (("s_dve", E.ct["s_dve"]), ("s_pool", E.ct["s_pool"]))
            if upto < 3:
                continue

            for c in range(16):
                for tt in range(4):
                    bank = ps[(c * 4 + tt) % 2]
                    E.emit("tensor", lambda e, b=bank, c=c, tt=tt, wd=wdt_bf, da=dtr_aug:
                           e.matmul(out=b[:, :], lhsT=wd[0:65, P * c:P * (c + 1)],
                                    rhs=da[0:65, 512 * tt:512 * (tt + 1)],
                                    start=True, stop=True),
                           waits=tuple(list(dtr_ok) + [("s_dve", E.c("s_dve"))]),
                           inc="s_pe")
                    mmv = E.ct["s_pe"]
                    E.emit("scalar", lambda e, b=bank, o=tz: e.activation(
                        out=o, in_=b[:, :], func=AF.Abs),
                        waits=(("s_pe", mmv), ("s_act", E.c("s_act")),
                               ("s_dve", E.c("s_dve"))), inc="s_act")
                    E.emit("vector", lambda e, b=bank, o=mx: e.tensor_scalar_max(
                        out=o, in0=b[:, :], scalar1=0.0),
                        waits=(("s_act", E.ct["s_act"]), ("s_dve", E.c("s_dve"))),
                        inc="s_dve")
                    E.emit("scalar", lambda e, o=tz: e.activation(
                        out=o, in_=o, func=AF.Exp, scale=-1.0),
                        waits=(("s_act", E.ct["s_act"]), ("s_dve", E.ct["s_dve"])),
                        inc="s_act")
                    E.emit("vector", lambda e, o=tz: e.tensor_scalar_add(
                        out=o, in0=o, scalar1=1.0),
                        waits=(("s_act", E.ct["s_act"]),), inc="s_dve")
                    E.emit("scalar", lambda e, o=tz: e.activation(
                        out=o, in_=o, func=AF.Ln),
                        waits=(("s_dve", E.ct["s_dve"]),), inc="s_act")
                    E.emit("vector", lambda e, o=dt_tile, a=mx, b2=tz: e.tensor_tensor(
                        out=o, in0=a, in1=b2, op=ALU.add),
                        waits=(("s_act", E.ct["s_act"]), ("d_dt", E.c("d_dt"))),
                        inc="s_dve")
                    E.dma("gpsimd", dtS[s, c, :, 512 * tt:512 * (tt + 1)],
                          dt_tile, inc="d_dt", waits=(("s_dve", E.ct["s_dve"]),))
            if upto < 4:
                continue

            E.emit("vector", lambda e, o=carry: e.memset(o, 0.0),
                   waits=(("s_dve", E.c("s_dve")),), inc="s_dve")
            E.emit("vector", lambda e, o=bc_bf, i=x_dbl: e.tensor_copy(
                out=o[64:96, :], in_=i[64:96, :]),
                waits=(("s_pe", E.c("s_pe")), ("s_dve", E.c("s_dve")),
                       ("d_c", E.c("d_c"))), inc="s_dve")
            bc_ok = E.ct["s_dve"]
            for th in range(4):
                th0 = QW * th
                for n in range(DS):
                    for (srcrow, dstv) in ((n, Bbc[n]), (16 + n, Cbc[n])):
                        E.emit("tensor", lambda e, o=ps[4][0:1, :], r=bc_bf, sr=srcrow,
                               th0=th0, idn=C["identb"]: e.matmul(
                                   out=o, lhsT=idn[64:96, 64 + sr:64 + sr + 1],
                                   rhs=r[64:96, th0:th0 + QW],
                                   start=True, stop=True),
                               waits=(("s_dve", E.c("s_dve")),
                                      ("s_act", E.c("s_act"))),
                               inc="s_pe")
                        E.emit("scalar", lambda e, o=brow_bf, i=ps[4][0:1, :]: e.copy(
                            out=o, in_=i),
                            waits=(("s_pe", E.ct["s_pe"]),), inc="s_act")
                        E.emit("tensor", lambda e, o=ps[4][:, :], r=brow_bf,
                               l=C["ones_row1"]: e.matmul(
                                   out=o, lhsT=l, rhs=r, start=True, stop=True),
                               waits=(("s_act", E.ct["s_act"]),), inc="s_pe")
                        E.emit("scalar", lambda e, o=dstv, i=ps[4][:, :]: e.copy(
                            out=o, in_=i),
                            waits=(("s_pe", E.ct["s_pe"]), ("s_act", E.ct["s_act"])),
                            inc="s_act")
                bc_done = (("s_act", E.ct["s_act"]),)
                for c in range(16):
                    E.dma("sync", dt_tile, dtS[s, c, :, th0:th0 + QW],
                          inc="d_in", waits=(("s_dve", E.c("s_dve")),
                                             ("d_dt", E.c("d_dt")),
                                             ("s_act", E.c("s_act"))))
                    E.dma("sync", sres_tile, sresS[s, c, :, th0:th0 + QW],
                          inc="d_in", waits=(("d_sr", E.c("d_sr")),
                                             ("s_dve", E.c("s_dve"))))
                    dtin = E.ct["d_in"]
                    E.emit("vector", lambda e, o=ubuf, a=dt_tile, xs=XH[c], th0=th0:
                           e.tensor_tensor(out=o, in0=a, in1=xs[:, th0:th0 + QW],
                                           op=ALU.mult),
                           waits=(("d_in", dtin),), inc="s_dve")
                    u_ok = E.ct["s_dve"]
                    for n in range(DS):
                        E.emit("scalar", lambda e, o=abuf, i=dt_tile, A=A_sb, c=c, n=n:
                               e.activation(out=o, in_=i, func=AF.Exp,
                                            scale=A[:, 16 * c + n:16 * c + n + 1]),
                               waits=(("d_in", dtin), ("s_dve", E.c("s_dve"))),
                               inc="s_act")
                        a_ok = E.ct["s_act"]
                        E.emit("vector", lambda e, o=tmpy, u=ubuf, b=Bbc[n]:
                               e.tensor_tensor(out=o, in0=u, in1=b, op=ALU.mult),
                               waits=(("s_dve", E.c("s_dve")),) + bc_done, inc="s_dve")
                        E.emit("vector", lambda e, o=hbuf, a=abuf, b=tmpy, cr=carry,
                               ci=16 * c + n: e.tensor_tensor_scan(
                                   out=o, data0=a, data1=b, initial=cr[:, ci:ci + 1],
                                   op0=ALU.mult, op1=ALU.add),
                               waits=(("s_act", a_ok), ("s_dve", E.ct["s_dve"])),
                               inc="s_dve")
                        E.emit("vector", lambda e, o=carry, i=hbuf, ci=16 * c + n:
                               e.tensor_copy(out=o[:, ci:ci + 1], in_=i[:, QW - 1:QW]),
                               waits=(("s_dve", E.ct["s_dve"]),), inc="s_dve")
                        if n == 0:
                            E.emit("vector", lambda e, o=yacc, h=hbuf, cc=Cbc[n]:
                                   e.tensor_tensor(out=o, in0=h, in1=cc, op=ALU.mult),
                                   waits=(("s_dve", E.ct["s_dve"]),), inc="s_dve")
                        else:
                            E.emit("vector", lambda e, o=tmpy, h=hbuf, cc=Cbc[n]:
                                   e.tensor_tensor(out=o, in0=h, in1=cc, op=ALU.mult),
                                   waits=(("s_dve", E.ct["s_dve"]),), inc="s_dve")
                            E.emit("vector", lambda e, o=yacc, t=tmpy: e.tensor_tensor(
                                out=o, in0=o, in1=t, op=ALU.add),
                                waits=(("s_dve", E.ct["s_dve"]),), inc="s_dve")
                    E.emit("vector", lambda e, o=yacc, xs=XH[c], th0=th0, dp=dpar, c=c:
                           e.scalar_tensor_tensor(out=o, in0=xs[:, th0:th0 + QW],
                                                  scalar=dp[:, c:c + 1], in1=o,
                                                  op0=ALU.mult, op1=ALU.add),
                           waits=(("s_dve", E.ct["s_dve"]),), inc="s_dve")
                    E.emit("vector", lambda e, o=XH[c], th0=th0, y=yacc, sr=sres_tile:
                           e.tensor_tensor(out=o[:, th0:th0 + QW], in0=y,
                                           in1=sr, op=ALU.mult),
                           waits=(("s_dve", E.ct["s_dve"]),), inc="s_dve")
            scan_done = (("s_dve", E.c("s_dve")),)
            if upto < 5:
                continue

            for tt in range(4):
                for oc in range(NCH):
                    wv16, _, cnm16, cv16, par16 = wl.load(woutT[:, P * oc:P * (oc + 1)],
                                                          None, 16, P)
                    bank = ps[oc % 4]
                    E.dma("sync", xres_t, xh_in[s, oc, :, 512 * tt:512 * (tt + 1)],
                          inc="d_in", waits=(("s_dve", E.c("s_dve")),))
                    xrv = E.ct["d_in"]
                    for dc in range(16):
                        w = (tuple(scan_done) + ((cnm16, cv16),
                             ("s_dve", E.c("s_dve")), ("s_act", E.c("s_act")))) \
                            if dc == 0 else ()
                        E.emit("tensor", lambda e, b=bank, wv=wv16, dc=dc, tt=tt,
                               xs=XH[dc]: e.matmul(
                                   out=b[:, :], lhsT=wv[:, dc, :],
                                   rhs=xs[:, 512 * tt:512 * (tt + 1)],
                                   start=(dc == 0), stop=(dc == 15)),
                               waits=w, inc="s_pe" if dc == 15 else None)
                    E.emit("vector", lambda e, b=bank, o=sum_bf[oc], xr=xres_t,
                           ac=alpha_col: e.scalar_tensor_tensor(
                               out=o, in0=b[:, :], scalar=ac, in1=xr,
                               op0=ALU.mult, op1=ALU.add),
                           waits=(("s_pe", E.ct["s_pe"]), ("d_in", xrv)), inc="s_dve")
                    wl.done(par16)
                out_t = [R2[4 + (i // 2)][:, 1024 + QW * (i % 2):1024 + QW * (i % 2 + 1)]
                         for i in range(8)]
                sr = (("s_dve", E.c("s_dve")), ("s_act", E.c("s_act")),
                      ("d_out", E.c("d_out")))
                emit_ln_tile(E, C, sum_bf, out_t, sq5, ps[4][:, :], ps[5][:, :],
                             ps[6][:, :], sc5, sum_ready=sr)
                ln_done = E.c("s_dve")
                for c in range(NCH):
                    E.dma("gpsimd", xh_out[s, c, :, 512 * tt:512 * (tt + 1)], out_t[c],
                          inc="d_out", waits=(("s_dve", ln_done),))
        E.wait("gpsimd", "d_out", E.c("d_out"))

        block = ctx.enter_context(nc.Block())
        E.flush(block)
    return nc



DIN = 2816
NPAIR = 22
WQ = 8.0


def host_constants_D():
    c = {}
    n = T
    half = np.arange(0, HD, 2, dtype=np.float64)
    inv_freq = 1.0 / 10000.0 ** (half / HD)
    t = np.arange(n, dtype=np.float64)
    freqs = t[:, None] * inv_freq[None, :]
    freqs = np.concatenate([freqs, freqs], -1)
    base_scale = (half + 0.4 * HD) / (1.4 * HD)
    power = (t - n // 2) / (256 // 2)
    scale = base_scale[None, :] ** power[:, None]
    scale = np.concatenate([scale, scale], -1)
    cos, sin = np.cos(freqs), np.sin(freqs)
    def fm(tab):
        return np.ascontiguousarray(
            np.tile(tab.T, (2, 1)).astype(BF))
    c["D_csq"] = fm(cos * scale)
    c["D_snq"] = fm(sin * scale)
    c["D_csk"] = fm(cos / scale)
    c["D_snk"] = fm(sin / scale)
    pm = np.zeros((P, P), np.float64)
    for blk in range(2):
        o = 64 * blk
        for d2 in range(32):
            pm[o + d2 + 32, o + d2] = -1.0
            pm[o + d2, o + d2 + 32] = 1.0
    c["D_rotperm"] = np.ascontiguousarray(pm.astype(BF))
    qsel = np.zeros((P, NCH * 16), np.float64)
    for ch in range(NCH):
        for p in range(P):
            qsel[p, 16 * ch + 2 * ch + p // 64] = 1.0
    c["D_qsel"] = np.ascontiguousarray(qsel.astype(BF))
    qbro = np.zeros((16, NCH * P), np.float64)
    for ch in range(NCH):
        for p in range(P):
            qbro[2 * ch + p // 64, P * ch + p] = 1.0
    c["D_qbro"] = np.ascontiguousarray(qbro.astype(BF))
    rep = np.zeros((HD, P), np.float64)
    for p in range(P):
        rep[p % HD, p] = 1.0
    c["D_rep64"] = np.ascontiguousarray(rep.astype(BF))
    q = np.arange(P)[:, None]
    k = np.arange(P)[None, :]
    c["D_mTadd"] = _f32(np.where(k.T > q.T * 0 + q.T - (q.T - q.T), 0.0, 0.0))
    c["D_mTadd"] = _f32(np.where(np.arange(P)[None, :] < np.arange(P)[:, None], NEG, 0.0))
    c["D_mT2add"] = _f32(np.where(np.arange(P)[None, :] > np.arange(P)[:, None], NEG, 0.0))
    c["D_mWadd"] = _f32(np.where(np.arange(P)[:, None] > np.arange(P)[None, :], NEG, 0.0))
    return c


def prep_weights_D(inp):
    w = {}
    w["D_wqkv"] = _f32(inp["L1_ffn_Wqkv"])
    w["D_wo"] = _f32(inp["L1_ffn_Wo"])
    W1 = _f32(inp["L1_ffn_W1"])
    b1 = _f32(inp["L1_ffn_b1"])
    w1p = np.zeros((6, D + 1, 2 * DIN), np.float32)
    for i in range(6):
        a = np.concatenate([W1[i][:, :2730], b1[i][None, :2730]], 0)
        g = np.concatenate([W1[i][:, 2730:], b1[i][None, 2730:]], 0)
        ap = np.zeros((D + 1, DIN), np.float32); ap[:, :2730] = a
        gp = np.zeros((D + 1, DIN), np.float32); gp[:, :2730] = g
        for op in range(NPAIR):
            w1p[i][:, 256 * op:256 * op + 128] = ap[:, 128 * op:128 * (op + 1)]
            w1p[i][:, 256 * op + 128:256 * (op + 1)] = gp[:, 128 * op:128 * (op + 1)]
    w["D_w1p"] = _f32(w1p)
    W2 = _f32(inp["L1_ffn_W2"])
    b2 = _f32(inp["L1_ffn_b2"])
    w2p = np.zeros((6, DIN + 1, D), np.float32)
    w2p[:, :2730, :] = W2
    w2p[:, DIN, :] = b2
    w["D_w2p"] = _f32(w2p)
    w["D_lng"] = _f32(_f32(inp["L1_ffn_ln_g"]).reshape(6, NCH, P)
                      .transpose(2, 0, 1).reshape(P, 48))
    w["D_lnb"] = _f32(_f32(inp["L1_ffn_ln_b"]).reshape(6, NCH, P)
                      .transpose(2, 0, 1).reshape(P, 48))
    w["D_qscaleT"] = _f32(_f32(inp["L1_ffn_q_scale"]).T)
    w["D_kscaleT"] = _f32(_f32(inp["L1_ffn_k_scale"]).T)
    return w


def build_D(race_detect=False, upto=99):
    nc = bass.Bass("TRN2", target_bir_lowering=False, detect_race_conditions=race_detect)
    xh_in = nc.dram_tensor("xh", [NSEQ, NCH, P, T], BF16, kind="ExternalInput").ap()
    wqkvD = nc.dram_tensor("D_wqkv", [6, D, 3 * D], F32, kind="ExternalInput").ap()
    woD = nc.dram_tensor("D_wo", [6, D, D], F32, kind="ExternalInput").ap()
    w1D = nc.dram_tensor("D_w1p", [6, D + 1, 2 * DIN], F32, kind="ExternalInput").ap()
    w2D = nc.dram_tensor("D_w2p", [6, DIN + 1, D], F32, kind="ExternalInput").ap()
    lngD = nc.dram_tensor("D_lng", [P, 48], F32, kind="ExternalInput").ap()
    lnbD = nc.dram_tensor("D_lnb", [P, 48], F32, kind="ExternalInput").ap()
    qscD = nc.dram_tensor("D_qscaleT", [HD, 6], F32, kind="ExternalInput").ap()
    kscD = nc.dram_tensor("D_kscaleT", [HD, 6], F32, kind="ExternalInput").ap()
    csqD = nc.dram_tensor("D_csq", [P, T], BF16, kind="ExternalInput").ap()
    snqD = nc.dram_tensor("D_snq", [P, T], BF16, kind="ExternalInput").ap()
    cskD = nc.dram_tensor("D_csk", [P, T], BF16, kind="ExternalInput").ap()
    snkD = nc.dram_tensor("D_snk", [P, T], BF16, kind="ExternalInput").ap()
    rotD = nc.dram_tensor("D_rotperm", [P, P], BF16, kind="ExternalInput").ap()
    qselD = nc.dram_tensor("D_qsel", [P, NCH * 16], BF16, kind="ExternalInput").ap()
    qbroD = nc.dram_tensor("D_qbro", [16, NCH * P], BF16, kind="ExternalInput").ap()
    rep64D = nc.dram_tensor("D_rep64", [HD, P], BF16, kind="ExternalInput").ap()
    mTaD = nc.dram_tensor("D_mTadd", [P, P], F32, kind="ExternalInput").ap()
    mT2aD = nc.dram_tensor("D_mT2add", [P, P], F32, kind="ExternalInput").ap()
    mWaD = nc.dram_tensor("D_mWadd", [P, P], F32, kind="ExternalInput").ap()
    mdiagD = nc.dram_tensor("mask_diag_add", [P, P], F32, kind="ExternalInput").ap()
    identD = nc.dram_tensor("ident_bf", [P, P], BF16, kind="ExternalInput").ap()
    identfD = nc.dram_tensor("ident_f32", [P, P], F32, kind="ExternalInput").ap()
    y_out = nc.dram_tensor("y", [NSEQ, T, D], F32, kind="ExternalOutput").ap()

    with ExitStack() as ctx:
        E = Em(nc, ctx)
        ab = Arena(nc, ctx, "ab", 64576, BF16)
        af = Arena(nc, ctx, "af", 15872, F32)
        C = {}
        C["ones_col"] = ab.alloc(64)[:, 0:1]
        C["ones_row1"] = ab.alloc(P)[0:1, 0:P]
        C["ones_r512"] = ab.alloc(512)[0:1, :]
        C["ones_1x64"] = ab.alloc(64)[0:1, :]
        for nm in ("ones_col", "ones_row1", "ones_r512", "ones_1x64"):
            E.emit("gpsimd", lambda e, v=C[nm]: e.memset(v, 1.0), inc="s_pool")
        C["ready_pool"] = E.ct["s_pool"]

        x_f = [ab.alloc(T + 64) for _ in range(NCH)]
        REG = ab.alloc(30720)
        Af = [REG[:, 2048 * c:2048 * (c + 1)] for c in range(NCH)]
        tbl_cs = REG[:, 16384:18432]
        tbl_sn = REG[:, 18432:20480]
        qaug = REG[0:HD + 1, 20480:22528]
        kaug = REG[0:HD + 1, 22528:24576]
        kaug_full = REG[:, 22528:24576]
        vtok = REG[:, 24576:26656]
        pT = REG[:, 26656:27168]
        rnB_sb = REG[:, 27168:27680]
        rn_bf = REG[0:16, 27680:29728]
        HB = [REG[:, 1024 * op:1024 * (op + 1)] for op in range(NPAIR)]
        xn = [REG[:, 22528 + 1024 * c:22528 + 1024 * (c + 1)] for c in range(NCH)]
        q_cur = ab.alloc(T)
        k_cur = ab.alloc(T)
        rotp = ab.alloc(P)
        qsel = ab.alloc(NCH * 16)
        qbro = ab.alloc(NCH * P)[0:16, :]
        rep64 = ab.alloc(P)[0:HD, :]
        identb = ab.alloc(P)
        recB_sb = ab.alloc(P)
        rec_b = ab.alloc(P)[0:1, :]
        mrow_bf = ab.alloc(P)[0:16, :]
        sq_t = ab.alloc(512)
        gl_t = ab.alloc(512)
        t2_t = ab.alloc(512)
        sq_scr = ab.alloc(512)
        sc = {"mean_b": ab.alloc(512)[0:1, :], "rstd_b": ab.alloc(512)[0:1, :]}
        xin_t = ab.alloc(512)
        qsc_bf = ab.alloc(64)[0:HD, 0:6]
        ksc_bf = ab.alloc(64)[0:HD, 0:6]

        lng = af.alloc(48)
        lnb = af.alloc(48)
        qs_col = af.alloc(64)[:, 0:1]
        ks_col = af.alloc(64)[:, 0:1]
        qsc_sb = af.alloc(64)[0:HD, 0:6]
        ksc_sb = af.alloc(64)[0:HD, 0:6]
        mq = af.alloc(16)[:, 0:16]
        rn_f = af.alloc(T)[0:16, :]
        rec_f = af.alloc(P)[0:1, :]
        mTa = af.alloc(P)
        mT2a = af.alloc(P)
        mWa = af.alloc(P)
        mdiag = af.alloc(P)
        identf = af.alloc(P)
        ost_f = af.alloc(1024)
        sc["mean2"] = af.alloc(512)[0:1, :]
        sc["var"] = af.alloc(512)[0:1, :]

        wl = WLoader(E, ab, af, 2816, 256, "w")
        wl2 = wl

        ps = [ctx.enter_context(nc.psum_tensor(f"ps{i}", [P, 512], F32)) for i in range(7)]
        psb = ctx.enter_context(nc.psum_tensor("psb", [P, 512], BF16))

        for (dst, srcD) in ((rotp, rotD), (qsel, qselD), (qbro, qbroD),
                            (rep64, rep64D), (identb, identD)):
            E.dma("sync", dst, srcD[:, :], inc="d_c")
        for (dst, srcD) in ((lng, lngD), (lnb, lnbD), (qsc_sb, qscD), (ksc_sb, kscD),
                            (mTa, mTaD), (mT2a, mT2aD), (mWa, mWaD), (mdiag, mdiagD),
                            (identf, identfD)):
            E.dma("sync", dst, srcD[:, :], inc="d_c")
        consts_ok = (("d_c", E.ct["d_c"]), ("s_pool", E.ct["s_pool"]))
        E.emit("gpsimd", lambda e, o=qsc_bf, i=qsc_sb: e.tensor_copy(out=o, in_=i),
               waits=consts_ok, inc="s_pool")
        E.emit("gpsimd", lambda e, o=ksc_bf, i=ksc_sb: e.tensor_copy(out=o, in_=i),
               inc="s_pool")
        consts_ok = (("d_c", E.ct["d_c"]), ("s_pool", E.ct["s_pool"]))

        for s in range(NSEQ):
            xw = [("s_dve", E.c("s_dve")), ("s_pe", E.c("s_pe")),
                  ("s_act", E.c("s_act")), ("d_out", E.c("d_out"))]
            for c in range(NCH):
                E.dma("sync", x_f[c][:, 1:1 + T], xh_in[s, c], inc="d_in",
                      waits=tuple(xw) if c == 0 else ())
            E.emit("gpsimd", lambda e: [e.memset(x_f[c][:, 0:1], 0.0)
                                        for c in range(NCH)][-1],
                   waits=(("d_in", E.ct["d_in"]),), inc="s_pool")
            x_ready = (("d_in", E.ct["d_in"]), ("s_pool", E.ct["s_pool"])) + consts_ok

            for li in range(6):
                shx = [x_f[c][:, 1:1 + T] if c < 4 else x_f[c][:, 0:T]
                       for c in range(NCH)]
                E.emit("tensor", lambda e, o=ps[6][:, 0:1], l=rep64, r=qsc_bf, li=li:
                       e.matmul(out=o, lhsT=l, rhs=r[:, li:li + 1], start=True, stop=True),
                       waits=tuple(x_ready) + (("s_act", E.c("s_act")),
                                               ("s_dve", E.c("s_dve"))), inc="s_pe")
                E.emit("scalar", lambda e, o=qs_col, i=ps[6][:, 0:1]: e.copy(out=o, in_=i),
                       waits=(("s_pe", E.ct["s_pe"]),), inc="s_act")
                E.emit("tensor", lambda e, o=ps[6][:, 0:1], l=rep64, r=ksc_bf, li=li:
                       e.matmul(out=o, lhsT=l, rhs=r[:, li:li + 1], start=True, stop=True),
                       waits=(("s_act", E.ct["s_act"]),), inc="s_pe")
                E.emit("scalar", lambda e, o=ks_col, i=ps[6][:, 0:1]: e.copy(out=o, in_=i),
                       waits=(("s_pe", E.ct["s_pe"]),), inc="s_act")
                qsks_ok = (("s_act", E.ct["s_act"]),)

                for h in range(NH):
                    ch, rb = h // 2, HD * (h % 2)
                    if h % 2 == 0:
                        for (which, dst, csD_, snD_, s_col) in (
                                ("q", q_cur, csqD, snqD, qs_col),
                                ("k", k_cur, cskD, snkD, ks_col)):
                            coff = 0 if which == "q" else D
                            cs_t, sn_t = tbl_cs, tbl_sn
                            E.dma("sync", tbl_cs, csD_[:, :], inc="d_tb",
                                  waits=(("s_dve", E.c("s_dve")),))
                            E.dma("sync", tbl_sn, snD_[:, :], inc="d_tb")
                            tbl_ok = E.ct["d_tb"]
                            norm_ps = ps[5][0:16, :]
                            def pdst(oc, tt, bank, pev, dst=dst, ch=ch, np_=norm_ps):
                                E.emit("scalar", lambda e, b=bank, o=dst, tt=tt: e.copy(
                                    out=o[:, 512 * tt:512 * (tt + 1)], in_=b[:, :]),
                                    waits=(("s_pe", pev), ("s_dve", E.c("s_dve"))),
                                    inc="s_act")
                                av = E.ct["s_act"]
                                E.emit("vector", lambda e, o=sq_t, i=dst, tt=tt:
                                       e.tensor_tensor(
                                           out=o, in0=i[:, 512 * tt:512 * (tt + 1)],
                                           in1=i[:, 512 * tt:512 * (tt + 1)],
                                           op=ALU.mult),
                                       waits=(("s_act", av),), inc="s_dve")
                                E.emit("tensor", lambda e, o=np_, l=qsel, ch=ch, tt=tt:
                                       e.matmul(out=o, lhsT=l[:, 16 * ch:16 * (ch + 1)],
                                                rhs=sq_t, start=True, stop=True),
                                       waits=(("s_dve", E.ct["s_dve"]),), inc="s_pe")
                                E.emit("scalar", lambda e, o=rn_f, i=np_, tt=tt:
                                       e.activation(out=o[0:16, 512 * tt:512 * (tt + 1)],
                                                    in_=i, func=AF.Sqrt),
                                       waits=(("s_pe", E.ct["s_pe"]),), inc="s_act")
                                E.emit("vector", lambda e, o=rn_f, tt=tt:
                                       e.tensor_scalar_max(
                                           out=o[0:16, 512 * tt:512 * (tt + 1)],
                                           in0=o[0:16, 512 * tt:512 * (tt + 1)],
                                           scalar1=1e-12),
                                       waits=(("s_act", E.ct["s_act"]),), inc="s_dve")
                                E.emit("vector", lambda e, o=rn_f, tt=tt: e.reciprocal(
                                    out=o[0:16, 512 * tt:512 * (tt + 1)],
                                    in_=o[0:16, 512 * tt:512 * (tt + 1)]),
                                    waits=(("s_dve", E.ct["s_dve"]),), inc="s_dve")
                                E.emit("vector", lambda e, o=rn_bf, i=rn_f, tt=tt:
                                       e.tensor_copy(
                                           out=o[0:16, 512 * tt:512 * (tt + 1)],
                                           in_=i[0:16, 512 * tt:512 * (tt + 1)]),
                                       waits=(("s_dve", E.ct["s_dve"]),), inc="s_dve")
                                return ("s_dve", E.ct["s_dve"])
                            emit_proj_fmajor(
                                E, C, wl, wqkvD[li][:, coff:coff + D], [ch], shx,
                                None, ps, [0, 1, 2, 3],
                                tuple(x_ready) + qsks_ok +
                                (("s_dve", E.c("s_dve")), ("s_act", E.c("s_act"))),
                                dst_fn=pdst, has_bias=False)
                            for tt in range(4):
                                sl = slice(512 * tt, 512 * (tt + 1))
                                E.emit("tensor", lambda e, o=ps[4][:, :], l=qbro, ch=ch,
                                       r=rn_bf, sl=sl: e.matmul(
                                           out=o, lhsT=l[:, P * ch:P * (ch + 1)],
                                           rhs=r[0:16, sl], start=True, stop=True),
                                       waits=(("s_dve", E.c("s_dve")),
                                              ("s_act", E.c("s_act"))), inc="s_pe")
                                E.emit("scalar", lambda e, o=rnB_sb, i=ps[4][:, :]:
                                       e.copy(out=o, in_=i),
                                       waits=(("s_pe", E.ct["s_pe"]),), inc="s_act")
                                E.emit("vector", lambda e, o=dst, sl=sl, sc2=s_col,
                                       rb2=rnB_sb: e.scalar_tensor_tensor(
                                           out=o[:, sl], in0=o[:, sl], scalar=sc2,
                                           in1=rb2, op0=ALU.mult, op1=ALU.mult),
                                       waits=(("s_act", E.ct["s_act"]),), inc="s_dve")
                                q1v = E.ct["s_dve"]
                                E.emit("tensor", lambda e, o=ps[4][:, :], l=rotp, i=dst,
                                       sl=sl: e.matmul(out=o, lhsT=l, rhs=i[:, sl],
                                                       start=True, stop=True),
                                       waits=(("s_dve", q1v), ("s_act", E.c("s_act"))),
                                       inc="s_pe")
                                E.emit("vector", lambda e, o=t2_t, i=ps[4][:, :],
                                       sn_t=sn_t, sl=sl: e.tensor_tensor(
                                           out=o, in0=i, in1=sn_t[:, sl], op=ALU.mult),
                                       waits=(("s_pe", E.ct["s_pe"]), ("d_tb", tbl_ok)),
                                       inc="s_dve")
                                E.emit("vector", lambda e, o=dst, sl=sl, cs_t=cs_t:
                                       e.tensor_tensor(out=o[:, sl], in0=o[:, sl],
                                                       in1=cs_t[:, sl], op=ALU.mult),
                                       waits=(("s_dve", E.ct["s_dve"]),), inc="s_dve")
                                E.emit("vector", lambda e, o=dst, sl=sl, t2=t2_t:
                                       e.tensor_tensor(out=o[:, sl], in0=o[:, sl],
                                                       in1=t2, op=ALU.add),
                                       waits=(("s_dve", E.ct["s_dve"]),), inc="s_dve")
                        E.emit("gpsimd", lambda e, v=vtok: e.memset(
                            v.rearrange("p (a h c) -> p a h c", a=16, h=2)
                            [:, :, :, HD:HD + 1], 1.0),
                            waits=(("s_pe", E.c("s_pe")),), inc="s_pool")
                        vone_ok = E.ct["s_pool"]
                        coff = 2 * D + P * ch
                        wv, _, cnm, cv, par = wl.load(wqkvD[li][:, coff:coff + P],
                                                      None, NCH, P)
                        for tc in range(16):
                            bank = ps[4 + tc % 2]
                            for dc in range(NCH):
                                w = ((cnm, cv), ("s_dve", E.c("s_dve")),
                                     ("s_act", E.c("s_act"))) if dc == 0 else ()
                                E.emit("tensor", lambda e, b=bank, wv=wv, dc=dc, tc=tc,
                                       xs=shx[dc]: e.matmul(
                                           out=b[:, 0:P], lhsT=xs[:, P * tc:P * (tc + 1)],
                                           rhs=wv[:, dc, :], start=(dc == 0),
                                           stop=(dc == NCH - 1)),
                                       waits=w, inc="s_pe" if dc == NCH - 1 else None)
                            E.emit("vector", lambda e, b=bank, tc=tc, v=vtok:
                                   e.tensor_copy(
                                       out=v.rearrange("p (a h c) -> p a h c", a=16, h=2)
                                           [:, tc, :, 0:HD],
                                       in_=b[:, 0:P].rearrange("p (h c) -> p h c", h=2)),
                                   waits=(("s_pe", E.ct["s_pe"]),
                                          ("s_pool", vone_ok)), inc="s_dve")
                        wl.done(par)
                    qk_done = (("s_dve", E.c("s_dve")),)
                    qrows = q_cur[rb:rb + HD, :]
                    krows = k_cur[rb:rb + HD, :]

                    p1_red = []
                    for j in range(16):
                        ks = P * max(0, j - 2)
                        wdt = P * (j + 1) - ks
                        bank = ps[j % 2]
                        bw = [p1_red[j - 2]] if j >= 2 else                             [("s_act", E.c("s_act")), ("s_dve", E.c("s_dve"))]
                        E.emit("tensor", lambda e, b=bank, j=j, ks=ks, wdt=wdt,
                               qr=qrows, kr=krows: e.matmul(
                                   out=b[:, 0:wdt], lhsT=qr[:, P * j:P * (j + 1)],
                                   rhs=kr[:, ks:ks + wdt], start=True, stop=True),
                               waits=tuple(list(qk_done) + bw), inc="s_pe")
                        mmv = E.ct["s_pe"]
                        if wdt == 384:
                            E.emit("vector", lambda e, b=bank, m=mWa: e.tensor_tensor(
                                out=b[:, 0:P], in0=b[:, 0:P], in1=m, op=ALU.add),
                                waits=(("s_pe", mmv),), inc="s_dve")
                        E.emit("vector", lambda e, b=bank, wdt=wdt, m=mdiag:
                               e.tensor_tensor(out=b[:, wdt - P:wdt],
                                               in0=b[:, wdt - P:wdt], in1=m, op=ALU.add),
                               waits=(("s_pe", mmv), ("s_dve", E.c("s_dve"))),
                               inc="s_dve")
                        E.emit("vector", lambda e, b=bank, j=j, wdt=wdt, m=mq:
                               e.reduce_max(out=m[:, j:j + 1], in_=b[:, 0:wdt],
                                            axis=AX.X),
                               waits=(("s_dve", E.ct["s_dve"]),), inc="s_dve")
                        p1_red.append(("s_dve", E.ct["s_dve"]))
                    E.emit("tensor", lambda e, m=mq, idn=identf, o=ps[6][0:16, 0:P]:
                           e.transpose(out=o, in_=m, identity=idn),
                           waits=(("s_dve", E.c("s_dve")),), inc="s_pe")
                    E.emit("scalar", lambda e, o=mrow_bf, i=ps[6][0:16, 0:P]: e.mul(
                        out=o, in_=i, mul=-1.0),
                        waits=(("s_pe", E.ct["s_pe"]),), inc="s_act")
                    E.dma("gpsimd", qaug[HD:HD + 1, :].rearrange("p (a b) -> p a b", a=16),
                          mrow_bf, inc="d_m",
                          waits=(("s_act", E.ct["s_act"]), ("s_pe", E.c("s_pe"))))
                    E.emit("vector", lambda e, o=qaug[0:HD, :], i=qrows: e.tensor_copy(
                        out=o, in_=i),
                        waits=(("s_pe", E.c("s_pe")),), inc="s_dve")
                    E.emit("vector", lambda e, o=kaug[0:HD, :], i=krows: e.tensor_copy(
                        out=o, in_=i), inc="s_dve")
                    if h == 0:
                        E.emit("gpsimd", lambda e, o=kaug[HD:HD + 1, :]: e.memset(o, 1.0),
                               waits=(("s_pe", E.c("s_pe")),), inc="s_pool")
                    aug_done = (("s_dve", E.ct["s_dve"]), ("d_m", E.c("d_m")),
                                ("s_pool", E.c("s_pool")))

                    exp_prev = [("s_act", E.c("s_act"))] * 2
                    for i in range(16):
                        q0 = P * i
                        qe = min(T, P * (i + 3))
                        wdt = qe - q0
                        bank = ps[i % 2]
                        E.emit("tensor", lambda e, b=bank, i=i, q0=q0, wdt=wdt,
                               ka=kaug, qa=qaug: e.matmul(
                                   out=b[:, 0:wdt], lhsT=ka[:, P * i:P * (i + 1)],
                                   rhs=qa[:, q0:q0 + wdt], start=True, stop=True),
                               waits=tuple(list(aug_done) + [exp_prev[i % 2]]),
                               inc="s_pe")
                        mmv = E.ct["s_pe"]
                        E.emit("vector", lambda e, b=bank, m=mTa: e.tensor_tensor(
                            out=b[:, 0:P], in0=b[:, 0:P], in1=m, op=ALU.add),
                            waits=(("s_pe", mmv), ("s_dve", E.c("s_dve"))), inc="s_dve")
                        if wdt == 384:
                            E.emit("vector", lambda e, b=bank, m=mT2a: e.tensor_tensor(
                                out=b[:, 256:384], in0=b[:, 256:384], in1=m, op=ALU.add),
                                waits=(("s_dve", E.ct["s_dve"]),), inc="s_dve")
                        E.emit("scalar", lambda e, b=bank, wdt=wdt, p=pT: e.activation(
                            out=p[:, 0:wdt], in_=b[:, 0:wdt], func=AF.Exp, scale=WQ),
                            waits=(("s_dve", E.ct["s_dve"]),), inc="s_act")
                        exp_prev[i % 2] = ("s_act", E.ct["s_act"])
                        exp_done = E.ct["s_act"]
                        for j in range(i, min(i + 3, 16)):
                            qoff = P * (j - i)
                            if P * j >= qe:
                                continue
                            ob = ps[2 + j % 3]
                            E.emit("tensor", lambda e, ob=ob, i=i, j=j, qoff=qoff, h=h,
                                   v=vtok, p=pT: e.matmul(
                                       out=ob[0:HD + 1, 0:P],
                                       lhsT=v[:, 130 * i + 65 * (h % 2):
                                              130 * i + 65 * (h % 2) + 65],
                                       rhs=p[:, qoff:qoff + P],
                                       start=(i == max(0, j - 2)), stop=(i == j)),
                                   waits=(("s_act", exp_done),) if j == i else (),
                                   inc="s_pe" if i == j else None)
                            if i == j:
                                pvv = E.ct["s_pe"]
                                E.emit("vector", lambda e, ob=ob, o=rec_f: e.tensor_copy(
                                    out=o[0:1, 0:P], in_=ob[HD:HD + 1, 0:P]),
                                    waits=(("s_pe", pvv), ("s_act", E.c("s_act"))),
                                    inc="s_dve")
                                E.emit("vector", lambda e, o=rec_f: e.reciprocal(
                                    out=o[0:1, 0:P], in_=o[0:1, 0:P]),
                                    waits=(("s_dve", E.ct["s_dve"]),), inc="s_dve")
                                E.emit("vector", lambda e, o=rec_b, i2=rec_f:
                                       e.tensor_copy(out=o, in_=i2[0:1, 0:P]),
                                       waits=(("s_dve", E.ct["s_dve"]),), inc="s_dve")
                                E.emit("tensor", lambda e, o=ps[6][0:HD, 0:P], r=rec_b,
                                       l=C["ones_1x64"]: e.matmul(
                                           out=o, lhsT=l, rhs=r, start=True, stop=True),
                                       waits=(("s_dve", E.ct["s_dve"]),
                                              ("s_act", E.c("s_act"))), inc="s_pe")
                                E.emit("scalar", lambda e, o=recB_sb[0:HD, 0:P],
                                       i2=ps[6][0:HD, 0:P]: e.copy(out=o, in_=i2),
                                       waits=(("s_pe", E.ct["s_pe"]),
                                              ("s_dve", E.c("s_dve"))), inc="s_act")
                                E.emit("vector", lambda e, ob=ob, j=j, ch=ch, rb=rb,
                                       o=Af[ch], rB=recB_sb: e.tensor_tensor(
                                           out=o[rb:rb + HD, P * j:P * (j + 1)],
                                           in0=ob[0:HD, 0:P], in1=rB[0:HD, 0:P],
                                           op=ALU.mult),
                                       waits=(("s_act", E.ct["s_act"]),), inc="s_dve")
                attn_done = (("s_dve", E.c("s_dve")), ("s_act", E.c("s_act")))

                def wodst(oc, tt, bank, pev):
                    return ("s_dve", E.emit(
                        "vector", lambda e, b=bank, o=x_f[oc][:, 1 + 512 * tt:
                                                              1 + 512 * (tt + 1)]:
                        e.tensor_tensor(out=o, in0=b[:, :], in1=o, op=ALU.add),
                        waits=(("s_pe", pev),), inc="s_dve"))
                emit_proj_fmajor(E, C, wl, woD[li], list(range(NCH)), Af, None, ps,
                                 [0, 1, 2, 3], attn_done, dst_fn=wodst, has_bias=False)

                gc = [lng[:, 8 * li + c:8 * li + c + 1] for c in range(NCH)]
                bc = [lnb[:, 8 * li + c:8 * li + c + 1] for c in range(NCH)]
                for half in (1, 0):
                    tts = [2 * half + 1, 2 * half]
                    for tt in tts:
                        sl = slice(512 * tt, 512 * (tt + 1))
                        shs = [x_f[c][:, 1 + 512 * tt:1 + 512 * (tt + 1)] if c < 4
                               else x_f[c][:, 512 * tt:512 * (tt + 1)]
                               for c in range(NCH)]
                        ho = 512 * (tt - 2 * half)
                        out_t = [xn[c][:, ho:ho + 512] for c in range(NCH)]
                        sr = (("s_dve", E.c("s_dve")), ("s_act", E.c("s_act")))
                        emit_ln_tile(E, C, shs, out_t, sq_scr, ps[4][:, :], ps[5][:, :],
                                     ps[6][:, :], sc, sum_ready=sr, gains=(gc, bc))
                    xn_done = (("s_dve", E.c("s_dve")),)
                    for op in range(NPAIR):
                        wv, bv, cnm, cv, par = wl.load(
                            w1D[li][0:D, 256 * op:256 * (op + 1)],
                            w1D[li][D:D + 1, 256 * op:256 * (op + 1)], NCH, 256)
                        for ti in range(2):
                            banka, bankg = ps[2 * (ti % 2)], ps[2 * (ti % 2) + 1]
                            for (bank, part) in ((banka, 0), (bankg, 1)):
                                for dc in range(NCH):
                                    w = (tuple(xn_done) + ((cnm, cv),
                                         ("s_dve", E.c("s_dve")),
                                         ("s_act", E.c("s_act")))) if dc == 0 else ()
                                    E.emit("tensor", lambda e, b=bank, wv=wv, dc=dc,
                                           ti=ti, part=part, x2=xn[dc]: e.matmul(
                                               out=b[:, :],
                                               lhsT=wv[:, dc, P * part:P * (part + 1)],
                                               rhs=x2[:, 512 * ti:512 * (ti + 1)],
                                               start=(dc == 0), stop=False),
                                           waits=w)
                                E.emit("tensor", lambda e, b=bank, bv=bv, part=part,
                                       o1=C["ones_r512"]: e.matmul(
                                           out=b[:, :],
                                           lhsT=bv[0:1, P * part:P * (part + 1)],
                                           rhs=o1, start=False, stop=True),
                                       inc="s_pe")
                            guv = E.ct["s_pe"]
                            E.emit("scalar", lambda e, o=gl_t, b=bankg: e.activation(
                                out=o, in_=b[:, :], func=AF.Gelu),
                                waits=(("s_pe", guv),), inc="s_act")
                            E.emit("vector", lambda e, o=HB[op], ti=ti, a=banka, g=gl_t:
                                   e.tensor_tensor(out=o[:, 512 * ti:512 * (ti + 1)],
                                                   in0=a[:, :], in1=g, op=ALU.mult),
                                   waits=(("s_pe", guv), ("s_act", E.ct["s_act"]),
                                          ("s_dve", E.c("s_dve"))), inc="s_dve")
                        wl.done(par)
                    hb_done = (("s_dve", E.c("s_dve")),)
                    for oc in range(NCH):
                        wv2, bv2, cnm2, cv2, par2 = wl2.load(
                            w2D[li][0:DIN, P * oc:P * (oc + 1)],
                            w2D[li][DIN:DIN + 1, P * oc:P * (oc + 1)], NPAIR, P)
                        for ti in (1, 0):
                            tt = 2 * half + ti
                            bank = ps[4 + (oc * 2 + ti) % 2]
                            for kc in range(NPAIR):
                                w = (tuple(hb_done) + ((cnm2, cv2),
                                     ("s_dve", E.c("s_dve")),
                                     ("s_act", E.c("s_act")))) if kc == 0 else ()
                                E.emit("tensor", lambda e, b=bank, wv=wv2, kc=kc, ti=ti,
                                       hh=HB[kc]: e.matmul(
                                           out=b[:, :], lhsT=wv[:, kc, :],
                                           rhs=hh[:, 512 * ti:512 * (ti + 1)],
                                           start=(kc == 0), stop=False),
                                       waits=w)
                            E.emit("tensor", lambda e, b=bank, bv=bv2,
                                   o1=C["ones_r512"]: e.matmul(
                                       out=b[:, :], lhsT=bv, rhs=o1,
                                       start=False, stop=True),
                                   inc="s_pe")
                            E.emit("vector", lambda e, b=bank,
                                   o=x_f[oc][:, 1 + 512 * tt:1 + 512 * (tt + 1)]:
                                   e.tensor_tensor(out=o, in0=b[:, :], in1=o,
                                                   op=ALU.add),
                                   waits=(("s_pe", E.ct["s_pe"]),), inc="s_dve")
                        wl2.done(par2)

            for tt in range(4):
                for c in range(NCH):
                    E.dma("sync", xin_t, xh_in[s, c, :, 512 * tt:512 * (tt + 1)],
                          inc="d_in", waits=(("s_dve", E.c("s_dve")),))
                    E.emit("vector", lambda e, o=Af[c], i=xin_t,
                           x2=x_f[c][:, 1 + 512 * tt:1 + 512 * (tt + 1)]:
                           e.tensor_tensor(out=o[:, 0:512], in0=i, in1=x2, op=ALU.add),
                           waits=(("d_in", E.ct["d_in"]),), inc="s_dve")
                sum_t = [Af[c][:, 0:512] for c in range(NCH)]
                out_t = [Af[c][:, 512:1024] for c in range(NCH)]
                sr = (("s_dve", E.c("s_dve")), ("s_act", E.c("s_act")))
                emit_ln_tile(E, C, sum_t, out_t, sq_scr, ps[4][:, :], ps[5][:, :],
                             ps[6][:, :], sc, sum_ready=sr)
                lnv = E.c("s_dve")
                for k4 in range(4):
                    for c in range(NCH):
                        E.emit("tensor", lambda e, o=psb[:, 0:P], i=out_t[c], k4=k4,
                               idn=identb: e.transpose(
                                   out=o, in_=i[:, P * k4:P * (k4 + 1)], identity=idn),
                               waits=(("s_dve", lnv), ("s_act", E.c("s_act")))
                               if c == 0 and k4 == 0 else
                               (("s_act", E.c("s_act")),), inc="s_pe")
                        E.emit("scalar", lambda e, o=ost_f, c=c, i=psb[:, 0:P]: e.copy(
                            out=o[:, P * c:P * (c + 1)], in_=i),
                            waits=(("s_pe", E.ct["s_pe"]),), inc="s_act")
                    E.dma("gpsimd", y_out[s, 512 * tt + P * k4:512 * tt + P * (k4 + 1), :],
                          ost_f, inc="d_out", waits=(("s_act", E.ct["s_act"]),))
        E.wait("gpsimd", "d_out", E.c("d_out"))

        block = ctx.enter_context(nc.Block())
        E.flush(block)
    return nc



DIN = 2816
NPAIR = 22
WQ = 8.0


def host_constants_D():
    c = {}
    n = T
    half = np.arange(0, HD, 2, dtype=np.float64)
    inv_freq = 1.0 / 10000.0 ** (half / HD)
    t = np.arange(n, dtype=np.float64)
    freqs = t[:, None] * inv_freq[None, :]
    freqs = np.concatenate([freqs, freqs], -1)
    base_scale = (half + 0.4 * HD) / (1.4 * HD)
    power = (t - n // 2) / (256 // 2)
    scale = base_scale[None, :] ** power[:, None]
    scale = np.concatenate([scale, scale], -1)
    cos, sin = np.cos(freqs), np.sin(freqs)
    def fm(tab):
        return np.ascontiguousarray(
            np.tile(tab.T, (2, 1)).astype(BF))
    c["D_csq"] = fm(cos * scale)
    c["D_snq"] = fm(sin * scale)
    c["D_csk"] = fm(cos / scale)
    c["D_snk"] = fm(sin / scale)
    pm = np.zeros((P, P), np.float64)
    for blk in range(2):
        o = 64 * blk
        for d2 in range(32):
            pm[o + d2 + 32, o + d2] = -1.0
            pm[o + d2, o + d2 + 32] = 1.0
    c["D_rotperm"] = np.ascontiguousarray(pm.astype(BF))
    qsel = np.zeros((P, NCH * 16), np.float64)
    for ch in range(NCH):
        for p in range(P):
            qsel[p, 16 * ch + 2 * ch + p // 64] = 1.0
    c["D_qsel"] = np.ascontiguousarray(qsel.astype(BF))
    qbro = np.zeros((16, NCH * P), np.float64)
    for ch in range(NCH):
        for p in range(P):
            qbro[2 * ch + p // 64, P * ch + p] = 1.0
    c["D_qbro"] = np.ascontiguousarray(qbro.astype(BF))
    rep = np.zeros((HD, P), np.float64)
    for p in range(P):
        rep[p % HD, p] = 1.0
    c["D_rep64"] = np.ascontiguousarray(rep.astype(BF))
    q = np.arange(P)[:, None]
    k = np.arange(P)[None, :]
    c["D_mTadd"] = _f32(np.where(k.T > q.T * 0 + q.T - (q.T - q.T), 0.0, 0.0))
    c["D_mTadd"] = _f32(np.where(np.arange(P)[None, :] < np.arange(P)[:, None], NEG, 0.0))
    c["D_mT2add"] = _f32(np.where(np.arange(P)[None, :] > np.arange(P)[:, None], NEG, 0.0))
    c["D_mWadd"] = _f32(np.where(np.arange(P)[:, None] > np.arange(P)[None, :], NEG, 0.0))
    return c


def prep_weights_D(inp):
    w = {}
    w["D_wqkv"] = _f32(inp["L1_ffn_Wqkv"])
    w["D_wo"] = _f32(inp["L1_ffn_Wo"])
    W1 = _f32(inp["L1_ffn_W1"])
    b1 = _f32(inp["L1_ffn_b1"])
    w1p = np.zeros((6, D + 1, 2 * DIN), np.float32)
    for i in range(6):
        a = np.concatenate([W1[i][:, :2730], b1[i][None, :2730]], 0)
        g = np.concatenate([W1[i][:, 2730:], b1[i][None, 2730:]], 0)
        ap = np.zeros((D + 1, DIN), np.float32); ap[:, :2730] = a
        gp = np.zeros((D + 1, DIN), np.float32); gp[:, :2730] = g
        for op in range(NPAIR):
            w1p[i][:, 256 * op:256 * op + 128] = ap[:, 128 * op:128 * (op + 1)]
            w1p[i][:, 256 * op + 128:256 * (op + 1)] = gp[:, 128 * op:128 * (op + 1)]
    w["D_w1p"] = _f32(w1p)
    W2 = _f32(inp["L1_ffn_W2"])
    b2 = _f32(inp["L1_ffn_b2"])
    w2p = np.zeros((6, DIN + 1, D), np.float32)
    w2p[:, :2730, :] = W2
    w2p[:, DIN, :] = b2
    w["D_w2p"] = _f32(w2p)
    w["D_lng"] = _f32(_f32(inp["L1_ffn_ln_g"]).reshape(6, NCH, P)
                      .transpose(2, 0, 1).reshape(P, 48))
    w["D_lnb"] = _f32(_f32(inp["L1_ffn_ln_b"]).reshape(6, NCH, P)
                      .transpose(2, 0, 1).reshape(P, 48))
    w["D_qscaleT"] = _f32(_f32(inp["L1_ffn_q_scale"]).T)
    w["D_kscaleT"] = _f32(_f32(inp["L1_ffn_k_scale"]).T)
    return w


def build_D(race_detect=False, upto=99):
    nc = bass.Bass("TRN2", target_bir_lowering=False, detect_race_conditions=race_detect)
    xh_in = nc.dram_tensor("xh", [NSEQ, NCH, P, T], BF16, kind="ExternalInput").ap()
    wqkvD = nc.dram_tensor("D_wqkv", [6, D, 3 * D], F32, kind="ExternalInput").ap()
    woD = nc.dram_tensor("D_wo", [6, D, D], F32, kind="ExternalInput").ap()
    w1D = nc.dram_tensor("D_w1p", [6, D + 1, 2 * DIN], F32, kind="ExternalInput").ap()
    w2D = nc.dram_tensor("D_w2p", [6, DIN + 1, D], F32, kind="ExternalInput").ap()
    lngD = nc.dram_tensor("D_lng", [P, 48], F32, kind="ExternalInput").ap()
    lnbD = nc.dram_tensor("D_lnb", [P, 48], F32, kind="ExternalInput").ap()
    qscD = nc.dram_tensor("D_qscaleT", [HD, 6], F32, kind="ExternalInput").ap()
    kscD = nc.dram_tensor("D_kscaleT", [HD, 6], F32, kind="ExternalInput").ap()
    csqD = nc.dram_tensor("D_csq", [P, T], BF16, kind="ExternalInput").ap()
    snqD = nc.dram_tensor("D_snq", [P, T], BF16, kind="ExternalInput").ap()
    cskD = nc.dram_tensor("D_csk", [P, T], BF16, kind="ExternalInput").ap()
    snkD = nc.dram_tensor("D_snk", [P, T], BF16, kind="ExternalInput").ap()
    rotD = nc.dram_tensor("D_rotperm", [P, P], BF16, kind="ExternalInput").ap()
    qselD = nc.dram_tensor("D_qsel", [P, NCH * 16], BF16, kind="ExternalInput").ap()
    qbroD = nc.dram_tensor("D_qbro", [16, NCH * P], BF16, kind="ExternalInput").ap()
    rep64D = nc.dram_tensor("D_rep64", [HD, P], BF16, kind="ExternalInput").ap()
    mTaD = nc.dram_tensor("D_mTadd", [P, P], F32, kind="ExternalInput").ap()
    mT2aD = nc.dram_tensor("D_mT2add", [P, P], F32, kind="ExternalInput").ap()
    mWaD = nc.dram_tensor("D_mWadd", [P, P], F32, kind="ExternalInput").ap()
    mdiagD = nc.dram_tensor("mask_diag_add", [P, P], F32, kind="ExternalInput").ap()
    identD = nc.dram_tensor("ident_bf", [P, P], BF16, kind="ExternalInput").ap()
    identfD = nc.dram_tensor("ident_f32", [P, P], F32, kind="ExternalInput").ap()
    y_out = nc.dram_tensor("y", [NSEQ, T, D], F32, kind="ExternalOutput").ap()

    with ExitStack() as ctx:
        E = Em(nc, ctx)
        ab = Arena(nc, ctx, "ab", 64576, BF16)
        af = Arena(nc, ctx, "af", 15872, F32)
        C = {}
        C["ones_col"] = ab.alloc(64)[:, 0:1]
        C["ones_row1"] = ab.alloc(P)[0:1, 0:P]
        C["ones_r512"] = ab.alloc(512)[0:1, :]
        C["ones_1x64"] = ab.alloc(64)[0:1, :]
        for nm in ("ones_col", "ones_row1", "ones_r512", "ones_1x64"):
            E.emit("gpsimd", lambda e, v=C[nm]: e.memset(v, 1.0), inc="s_pool")
        C["ready_pool"] = E.ct["s_pool"]

        x_f = [ab.alloc(T + 64) for _ in range(NCH)]
        REG = ab.alloc(30720)
        Af = [REG[:, 2048 * c:2048 * (c + 1)] for c in range(NCH)]
        tbl_cs = REG[:, 16384:18432]
        tbl_sn = REG[:, 18432:20480]
        qaug = REG[0:HD + 1, 20480:22528]
        kaug = REG[0:HD + 1, 22528:24576]
        kaug_full = REG[:, 22528:24576]
        vtok = REG[:, 24576:26656]
        pT = REG[:, 26656:27168]
        rnB_sb = REG[:, 27168:27680]
        rn_bf = REG[0:16, 27680:29728]
        HB = [REG[:, 1024 * op:1024 * (op + 1)] for op in range(NPAIR)]
        xn = [REG[:, 22528 + 1024 * c:22528 + 1024 * (c + 1)] for c in range(NCH)]
        q_cur = ab.alloc(T)
        k_cur = ab.alloc(T)
        rotp = ab.alloc(P)
        qsel = ab.alloc(NCH * 16)
        qbro = ab.alloc(NCH * P)[0:16, :]
        rep64 = ab.alloc(P)[0:HD, :]
        identb = ab.alloc(P)
        recB_sb = ab.alloc(P)
        rec_b = ab.alloc(P)[0:1, :]
        mrow_bf = ab.alloc(P)[0:16, :]
        sq_t = ab.alloc(512)
        gl_t = ab.alloc(512)
        t2_t = ab.alloc(512)
        sq_scr = ab.alloc(512)
        sc = {"mean_b": ab.alloc(512)[0:1, :], "rstd_b": ab.alloc(512)[0:1, :]}
        xin_t = ab.alloc(512)
        qsc_bf = ab.alloc(64)[0:HD, 0:6]
        ksc_bf = ab.alloc(64)[0:HD, 0:6]

        lng = af.alloc(48)[:, 0:48]
        lnb = af.alloc(48)[:, 0:48]
        qs_col = af.alloc(64)[:, 0:1]
        ks_col = af.alloc(64)[:, 0:1]
        qsc_sb = af.alloc(64)[0:HD, 0:6]
        ksc_sb = af.alloc(64)[0:HD, 0:6]
        mq = af.alloc(16)[:, 0:16]
        rn_f = af.alloc(T)[0:16, :]
        rec_f = af.alloc(P)[0:1, :]
        mTa = af.alloc(P)
        mT2a = af.alloc(P)
        mWa = af.alloc(P)
        mdiag = af.alloc(P)
        identf = af.alloc(P)
        ost_f = af.alloc(1024)
        sc["mean2"] = af.alloc(512)[0:1, :]
        sc["var"] = af.alloc(512)[0:1, :]

        wl = WLoader(E, ab, af, 2816, 256, "w")
        wl2 = wl

        ps = [ctx.enter_context(nc.psum_tensor(f"ps{i}", [P, 512], F32)) for i in range(7)]
        psb = ctx.enter_context(nc.psum_tensor("psb", [P, 512], BF16))

        for (dst, srcD) in ((rotp, rotD), (qsel, qselD), (qbro, qbroD),
                            (rep64, rep64D), (identb, identD)):
            E.dma("sync", dst, srcD[:, :], inc="d_c")
        for (dst, srcD) in ((lng, lngD), (lnb, lnbD), (qsc_sb, qscD), (ksc_sb, kscD),
                            (mTa, mTaD), (mT2a, mT2aD), (mWa, mWaD), (mdiag, mdiagD),
                            (identf, identfD)):
            E.dma("sync", dst, srcD[:, :], inc="d_c")
        consts_ok = (("d_c", E.ct["d_c"]), ("s_pool", E.ct["s_pool"]))
        E.emit("gpsimd", lambda e, o=qsc_bf, i=qsc_sb: e.tensor_copy(out=o, in_=i),
               waits=consts_ok, inc="s_pool")
        E.emit("gpsimd", lambda e, o=ksc_bf, i=ksc_sb: e.tensor_copy(out=o, in_=i),
               inc="s_pool")
        consts_ok = (("d_c", E.ct["d_c"]), ("s_pool", E.ct["s_pool"]))

        for s in range(NSEQ):
            xw = [("s_dve", E.c("s_dve")), ("s_pe", E.c("s_pe")),
                  ("s_act", E.c("s_act")), ("d_out", E.c("d_out"))]
            for c in range(NCH):
                E.dma("sync", x_f[c][:, 1:1 + T], xh_in[s, c], inc="d_in",
                      waits=tuple(xw) if c == 0 else ())
            E.emit("gpsimd", lambda e: [e.memset(x_f[c][:, 0:1], 0.0)
                                        for c in range(NCH)][-1],
                   waits=(("d_in", E.ct["d_in"]),), inc="s_pool")
            x_ready = (("d_in", E.ct["d_in"]), ("s_pool", E.ct["s_pool"])) + consts_ok

            for li in range(6):
                shx = [x_f[c][:, 1:1 + T] if c < 4 else x_f[c][:, 0:T]
                       for c in range(NCH)]
                E.emit("tensor", lambda e, o=ps[6][:, 0:1], l=rep64, r=qsc_bf, li=li:
                       e.matmul(out=o, lhsT=l, rhs=r[:, li:li + 1], start=True, stop=True),
                       waits=tuple(x_ready) + (("s_act", E.c("s_act")),
                                               ("s_dve", E.c("s_dve"))), inc="s_pe")
                E.emit("scalar", lambda e, o=qs_col, i=ps[6][:, 0:1]: e.copy(out=o, in_=i),
                       waits=(("s_pe", E.ct["s_pe"]),), inc="s_act")
                E.emit("tensor", lambda e, o=ps[6][:, 0:1], l=rep64, r=ksc_bf, li=li:
                       e.matmul(out=o, lhsT=l, rhs=r[:, li:li + 1], start=True, stop=True),
                       waits=(("s_act", E.ct["s_act"]),), inc="s_pe")
                E.emit("scalar", lambda e, o=ks_col, i=ps[6][:, 0:1]: e.copy(out=o, in_=i),
                       waits=(("s_pe", E.ct["s_pe"]),), inc="s_act")
                qsks_ok = (("s_act", E.ct["s_act"]),)

                for h in range(NH):
                    ch, rb = h // 2, HD * (h % 2)
                    if h % 2 == 0:
                        for (which, dst, csD_, snD_, s_col) in (
                                ("q", q_cur, csqD, snqD, qs_col),
                                ("k", k_cur, cskD, snkD, ks_col)):
                            coff = 0 if which == "q" else D
                            cs_t, sn_t = tbl_cs, tbl_sn
                            E.dma("sync", tbl_cs, csD_[:, :], inc="d_tb",
                                  waits=(("s_dve", E.c("s_dve")),))
                            E.dma("sync", tbl_sn, snD_[:, :], inc="d_tb")
                            tbl_ok = E.ct["d_tb"]
                            norm_ps = ps[5][0:16, :]
                            def pdst(oc, tt, bank, pev, dst=dst, ch=ch, np_=norm_ps):
                                E.emit("scalar", lambda e, b=bank, o=dst, tt=tt: e.copy(
                                    out=o[:, 512 * tt:512 * (tt + 1)], in_=b[:, :]),
                                    waits=(("s_pe", pev), ("s_dve", E.c("s_dve"))),
                                    inc="s_act")
                                av = E.ct["s_act"]
                                E.emit("vector", lambda e, o=sq_t, i=dst, tt=tt:
                                       e.tensor_tensor(
                                           out=o, in0=i[:, 512 * tt:512 * (tt + 1)],
                                           in1=i[:, 512 * tt:512 * (tt + 1)],
                                           op=ALU.mult),
                                       waits=(("s_act", av),), inc="s_dve")
                                E.emit("tensor", lambda e, o=np_, l=qsel, ch=ch, tt=tt:
                                       e.matmul(out=o, lhsT=l[:, 16 * ch:16 * (ch + 1)],
                                                rhs=sq_t, start=True, stop=True),
                                       waits=(("s_dve", E.ct["s_dve"]),), inc="s_pe")
                                E.emit("scalar", lambda e, o=rn_f, i=np_, tt=tt:
                                       e.activation(out=o[0:16, 512 * tt:512 * (tt + 1)],
                                                    in_=i, func=AF.Sqrt),
                                       waits=(("s_pe", E.ct["s_pe"]),), inc="s_act")
                                E.emit("vector", lambda e, o=rn_f, tt=tt:
                                       e.tensor_scalar_max(
                                           out=o[0:16, 512 * tt:512 * (tt + 1)],
                                           in0=o[0:16, 512 * tt:512 * (tt + 1)],
                                           scalar1=1e-12),
                                       waits=(("s_act", E.ct["s_act"]),), inc="s_dve")
                                E.emit("vector", lambda e, o=rn_f, tt=tt: e.reciprocal(
                                    out=o[0:16, 512 * tt:512 * (tt + 1)],
                                    in_=o[0:16, 512 * tt:512 * (tt + 1)]),
                                    waits=(("s_dve", E.ct["s_dve"]),), inc="s_dve")
                                E.emit("vector", lambda e, o=rn_bf, i=rn_f, tt=tt:
                                       e.tensor_copy(
                                           out=o[0:16, 512 * tt:512 * (tt + 1)],
                                           in_=i[0:16, 512 * tt:512 * (tt + 1)]),
                                       waits=(("s_dve", E.ct["s_dve"]),), inc="s_dve")
                                return ("s_dve", E.ct["s_dve"])
                            emit_proj_fmajor(
                                E, C, wl, wqkvD[li][:, coff:coff + D], [ch], shx,
                                None, ps, [0, 1, 2, 3],
                                tuple(x_ready) + qsks_ok +
                                (("s_dve", E.c("s_dve")), ("s_act", E.c("s_act"))),
                                dst_fn=pdst, has_bias=False)
                            for tt in range(4):
                                sl = slice(512 * tt, 512 * (tt + 1))
                                E.emit("tensor", lambda e, o=ps[4][:, :], l=qbro, ch=ch,
                                       r=rn_bf, sl=sl: e.matmul(
                                           out=o, lhsT=l[:, P * ch:P * (ch + 1)],
                                           rhs=r[0:16, sl], start=True, stop=True),
                                       waits=(("s_dve", E.c("s_dve")),
                                              ("s_act", E.c("s_act"))), inc="s_pe")
                                E.emit("scalar", lambda e, o=rnB_sb, i=ps[4][:, :]:
                                       e.copy(out=o, in_=i),
                                       waits=(("s_pe", E.ct["s_pe"]),), inc="s_act")
                                E.emit("vector", lambda e, o=dst, sl=sl, sc2=s_col,
                                       rb2=rnB_sb: e.scalar_tensor_tensor(
                                           out=o[:, sl], in0=o[:, sl], scalar=sc2,
                                           in1=rb2, op0=ALU.mult, op1=ALU.mult),
                                       waits=(("s_act", E.ct["s_act"]),), inc="s_dve")
                                q1v = E.ct["s_dve"]
                                E.emit("tensor", lambda e, o=ps[4][:, :], l=rotp, i=dst,
                                       sl=sl: e.matmul(out=o, lhsT=l, rhs=i[:, sl],
                                                       start=True, stop=True),
                                       waits=(("s_dve", q1v), ("s_act", E.c("s_act"))),
                                       inc="s_pe")
                                E.emit("vector", lambda e, o=t2_t, i=ps[4][:, :],
                                       sn_t=sn_t, sl=sl: e.tensor_tensor(
                                           out=o, in0=i, in1=sn_t[:, sl], op=ALU.mult),
                                       waits=(("s_pe", E.ct["s_pe"]), ("d_tb", tbl_ok)),
                                       inc="s_dve")
                                E.emit("vector", lambda e, o=dst, sl=sl, cs_t=cs_t:
                                       e.tensor_tensor(out=o[:, sl], in0=o[:, sl],
                                                       in1=cs_t[:, sl], op=ALU.mult),
                                       waits=(("s_dve", E.ct["s_dve"]),), inc="s_dve")
                                E.emit("vector", lambda e, o=dst, sl=sl, t2=t2_t:
                                       e.tensor_tensor(out=o[:, sl], in0=o[:, sl],
                                                       in1=t2, op=ALU.add),
                                       waits=(("s_dve", E.ct["s_dve"]),), inc="s_dve")
                        E.emit("gpsimd", lambda e, v=vtok: e.memset(
                            v.rearrange("p (a h c) -> p a h c", a=16, h=2)
                            [:, :, :, HD:HD + 1], 1.0),
                            waits=(("s_pe", E.c("s_pe")),), inc="s_pool")
                        vone_ok = E.ct["s_pool"]
                        coff = 2 * D + P * ch
                        wv, _, cnm, cv, par = wl.load(wqkvD[li][:, coff:coff + P],
                                                      None, NCH, P)
                        for tc in range(16):
                            bank = ps[4 + tc % 2]
                            for dc in range(NCH):
                                w = ((cnm, cv), ("s_dve", E.c("s_dve")),
                                     ("s_act", E.c("s_act"))) if dc == 0 else ()
                                E.emit("tensor", lambda e, b=bank, wv=wv, dc=dc, tc=tc,
                                       xs=shx[dc]: e.matmul(
                                           out=b[:, 0:P], lhsT=xs[:, P * tc:P * (tc + 1)],
                                           rhs=wv[:, dc, :], start=(dc == 0),
                                           stop=(dc == NCH - 1)),
                                       waits=w, inc="s_pe" if dc == NCH - 1 else None)
                            E.emit("vector", lambda e, b=bank, tc=tc, v=vtok:
                                   e.tensor_copy(
                                       out=v.rearrange("p (a h c) -> p a h c", a=16, h=2)
                                           [:, tc, :, 0:HD],
                                       in_=b[:, 0:P].rearrange("p (h c) -> p h c", h=2)),
                                   waits=(("s_pe", E.ct["s_pe"]),
                                          ("s_pool", vone_ok)), inc="s_dve")
                        wl.done(par)
                    qk_done = (("s_dve", E.c("s_dve")),)
                    qrows = q_cur[rb:rb + HD, :]
                    krows = k_cur[rb:rb + HD, :]

                    p1_red = []
                    for j in range(16):
                        ks = P * max(0, j - 2)
                        wdt = P * (j + 1) - ks
                        bank = ps[j % 2]
                        bw = [p1_red[j - 2]] if j >= 2 else                             [("s_act", E.c("s_act")), ("s_dve", E.c("s_dve"))]
                        E.emit("tensor", lambda e, b=bank, j=j, ks=ks, wdt=wdt,
                               qr=qrows, kr=krows: e.matmul(
                                   out=b[:, 0:wdt], lhsT=qr[:, P * j:P * (j + 1)],
                                   rhs=kr[:, ks:ks + wdt], start=True, stop=True),
                               waits=tuple(list(qk_done) + bw), inc="s_pe")
                        mmv = E.ct["s_pe"]
                        if wdt == 384:
                            E.emit("vector", lambda e, b=bank, m=mWa: e.tensor_tensor(
                                out=b[:, 0:P], in0=b[:, 0:P], in1=m, op=ALU.add),
                                waits=(("s_pe", mmv),), inc="s_dve")
                        E.emit("vector", lambda e, b=bank, wdt=wdt, m=mdiag:
                               e.tensor_tensor(out=b[:, wdt - P:wdt],
                                               in0=b[:, wdt - P:wdt], in1=m, op=ALU.add),
                               waits=(("s_pe", mmv), ("s_dve", E.c("s_dve"))),
                               inc="s_dve")
                        E.emit("vector", lambda e, b=bank, j=j, wdt=wdt, m=mq:
                               e.reduce_max(out=m[:, j:j + 1], in_=b[:, 0:wdt],
                                            axis=AX.X),
                               waits=(("s_dve", E.ct["s_dve"]),), inc="s_dve")
                        p1_red.append(("s_dve", E.ct["s_dve"]))
                    E.emit("tensor", lambda e, m=mq, idn=identf, o=ps[6][0:16, 0:P]:
                           e.transpose(out=o, in_=m, identity=idn),
                           waits=(("s_dve", E.c("s_dve")),), inc="s_pe")
                    E.emit("scalar", lambda e, o=mrow_bf, i=ps[6][0:16, 0:P]: e.mul(
                        out=o, in_=i, mul=-1.0),
                        waits=(("s_pe", E.ct["s_pe"]),), inc="s_act")
                    E.dma("gpsimd", qaug[HD:HD + 1, :].rearrange("p (a b) -> p a b", a=16),
                          mrow_bf, inc="d_m",
                          waits=(("s_act", E.ct["s_act"]), ("s_pe", E.c("s_pe"))))
                    E.emit("vector", lambda e, o=qaug[0:HD, :], i=qrows: e.tensor_copy(
                        out=o, in_=i),
                        waits=(("s_pe", E.c("s_pe")),), inc="s_dve")
                    E.emit("vector", lambda e, o=kaug[0:HD, :], i=krows: e.tensor_copy(
                        out=o, in_=i), inc="s_dve")
                    if h == 0:
                        E.emit("gpsimd", lambda e, o=kaug[HD:HD + 1, :]: e.memset(o, 1.0),
                               waits=(("s_pe", E.c("s_pe")),), inc="s_pool")
                    aug_done = (("s_dve", E.ct["s_dve"]), ("d_m", E.c("d_m")),
                                ("s_pool", E.c("s_pool")))

                    exp_prev = [("s_act", E.c("s_act"))] * 2
                    for i in range(16):
                        q0 = P * i
                        qe = min(T, P * (i + 3))
                        wdt = qe - q0
                        bank = ps[i % 2]
                        E.emit("tensor", lambda e, b=bank, i=i, q0=q0, wdt=wdt,
                               ka=kaug, qa=qaug: e.matmul(
                                   out=b[:, 0:wdt], lhsT=ka[:, P * i:P * (i + 1)],
                                   rhs=qa[:, q0:q0 + wdt], start=True, stop=True),
                               waits=tuple(list(aug_done) + [exp_prev[i % 2]]),
                               inc="s_pe")
                        mmv = E.ct["s_pe"]
                        E.emit("vector", lambda e, b=bank, m=mTa: e.tensor_tensor(
                            out=b[:, 0:P], in0=b[:, 0:P], in1=m, op=ALU.add),
                            waits=(("s_pe", mmv), ("s_dve", E.c("s_dve"))), inc="s_dve")
                        if wdt == 384:
                            E.emit("vector", lambda e, b=bank, m=mT2a: e.tensor_tensor(
                                out=b[:, 256:384], in0=b[:, 256:384], in1=m, op=ALU.add),
                                waits=(("s_dve", E.ct["s_dve"]),), inc="s_dve")
                        E.emit("scalar", lambda e, b=bank, wdt=wdt, p=pT: e.activation(
                            out=p[:, 0:wdt], in_=b[:, 0:wdt], func=AF.Exp, scale=WQ),
                            waits=(("s_dve", E.ct["s_dve"]),), inc="s_act")
                        exp_prev[i % 2] = ("s_act", E.ct["s_act"])
                        exp_done = E.ct["s_act"]
                        for j in range(i, min(i + 3, 16)):
                            qoff = P * (j - i)
                            if P * j >= qe:
                                continue
                            ob = ps[2 + j % 3]
                            E.emit("tensor", lambda e, ob=ob, i=i, j=j, qoff=qoff, h=h,
                                   v=vtok, p=pT: e.matmul(
                                       out=ob[0:HD + 1, 0:P],
                                       lhsT=v[:, 130 * i + 65 * (h % 2):
                                              130 * i + 65 * (h % 2) + 65],
                                       rhs=p[:, qoff:qoff + P],
                                       start=(i == max(0, j - 2)), stop=(i == j)),
                                   waits=(("s_act", exp_done),) if j == i else (),
                                   inc="s_pe" if i == j else None)
                            if i == j:
                                pvv = E.ct["s_pe"]
                                E.emit("vector", lambda e, ob=ob, o=rec_f: e.tensor_copy(
                                    out=o[0:1, 0:P], in_=ob[HD:HD + 1, 0:P]),
                                    waits=(("s_pe", pvv), ("s_act", E.c("s_act"))),
                                    inc="s_dve")
                                E.emit("vector", lambda e, o=rec_f: e.reciprocal(
                                    out=o[0:1, 0:P], in_=o[0:1, 0:P]),
                                    waits=(("s_dve", E.ct["s_dve"]),), inc="s_dve")
                                E.emit("vector", lambda e, o=rec_b, i2=rec_f:
                                       e.tensor_copy(out=o, in_=i2[0:1, 0:P]),
                                       waits=(("s_dve", E.ct["s_dve"]),), inc="s_dve")
                                E.emit("tensor", lambda e, o=ps[6][0:HD, 0:P], r=rec_b,
                                       l=C["ones_1x64"]: e.matmul(
                                           out=o, lhsT=l, rhs=r, start=True, stop=True),
                                       waits=(("s_dve", E.ct["s_dve"]),
                                              ("s_act", E.c("s_act"))), inc="s_pe")
                                E.emit("scalar", lambda e, o=recB_sb[0:HD, 0:P],
                                       i2=ps[6][0:HD, 0:P]: e.copy(out=o, in_=i2),
                                       waits=(("s_pe", E.ct["s_pe"]),
                                              ("s_dve", E.c("s_dve"))), inc="s_act")
                                E.emit("vector", lambda e, ob=ob, j=j, ch=ch, rb=rb,
                                       o=Af[ch], rB=recB_sb: e.tensor_tensor(
                                           out=o[rb:rb + HD, P * j:P * (j + 1)],
                                           in0=ob[0:HD, 0:P], in1=rB[0:HD, 0:P],
                                           op=ALU.mult),
                                       waits=(("s_act", E.ct["s_act"]),), inc="s_dve")
                attn_done = (("s_dve", E.c("s_dve")), ("s_act", E.c("s_act")))

                def wodst(oc, tt, bank, pev):
                    return ("s_dve", E.emit(
                        "vector", lambda e, b=bank, o=x_f[oc][:, 1 + 512 * tt:
                                                              1 + 512 * (tt + 1)]:
                        e.tensor_tensor(out=o, in0=b[:, :], in1=o, op=ALU.add),
                        waits=(("s_pe", pev),), inc="s_dve"))
                emit_proj_fmajor(E, C, wl, woD[li], list(range(NCH)), Af, None, ps,
                                 [0, 1, 2, 3], attn_done, dst_fn=wodst, has_bias=False)

                gc = [lng[:, 8 * li + c:8 * li + c + 1] for c in range(NCH)]
                bc = [lnb[:, 8 * li + c:8 * li + c + 1] for c in range(NCH)]
                for half in (1, 0):
                    tts = [2 * half + 1, 2 * half]
                    for tt in tts:
                        sl = slice(512 * tt, 512 * (tt + 1))
                        shs = [x_f[c][:, 1 + 512 * tt:1 + 512 * (tt + 1)] if c < 4
                               else x_f[c][:, 512 * tt:512 * (tt + 1)]
                               for c in range(NCH)]
                        ho = 512 * (tt - 2 * half)
                        out_t = [xn[c][:, ho:ho + 512] for c in range(NCH)]
                        sr = (("s_dve", E.c("s_dve")), ("s_act", E.c("s_act")))
                        emit_ln_tile(E, C, shs, out_t, sq_scr, ps[4][:, :], ps[5][:, :],
                                     ps[6][:, :], sc, sum_ready=sr, gains=(gc, bc))
                    xn_done = (("s_dve", E.c("s_dve")),)
                    for op in range(NPAIR):
                        wv, bv, cnm, cv, par = wl.load(
                            w1D[li][0:D, 256 * op:256 * (op + 1)],
                            w1D[li][D:D + 1, 256 * op:256 * (op + 1)], NCH, 256)
                        for ti in range(2):
                            banka, bankg = ps[2 * (ti % 2)], ps[2 * (ti % 2) + 1]
                            for (bank, part) in ((banka, 0), (bankg, 1)):
                                for dc in range(NCH):
                                    w = (tuple(xn_done) + ((cnm, cv),
                                         ("s_dve", E.c("s_dve")),
                                         ("s_act", E.c("s_act")))) if dc == 0 else ()
                                    E.emit("tensor", lambda e, b=bank, wv=wv, dc=dc,
                                           ti=ti, part=part, x2=xn[dc]: e.matmul(
                                               out=b[:, :],
                                               lhsT=wv[:, dc, P * part:P * (part + 1)],
                                               rhs=x2[:, 512 * ti:512 * (ti + 1)],
                                               start=(dc == 0), stop=False),
                                           waits=w)
                                E.emit("tensor", lambda e, b=bank, bv=bv, part=part,
                                       o1=C["ones_r512"]: e.matmul(
                                           out=b[:, :],
                                           lhsT=bv[0:1, P * part:P * (part + 1)],
                                           rhs=o1, start=False, stop=True),
                                       inc="s_pe")
                            guv = E.ct["s_pe"]
                            E.emit("scalar", lambda e, o=gl_t, b=bankg: e.activation(
                                out=o, in_=b[:, :], func=AF.Gelu),
                                waits=(("s_pe", guv),), inc="s_act")
                            E.emit("vector", lambda e, o=HB[op], ti=ti, a=banka, g=gl_t:
                                   e.tensor_tensor(out=o[:, 512 * ti:512 * (ti + 1)],
                                                   in0=a[:, :], in1=g, op=ALU.mult),
                                   waits=(("s_pe", guv), ("s_act", E.ct["s_act"]),
                                          ("s_dve", E.c("s_dve"))), inc="s_dve")
                        wl.done(par)
                    hb_done = (("s_dve", E.c("s_dve")),)
                    for oc in range(NCH):
                        wv2, bv2, cnm2, cv2, par2 = wl2.load(
                            w2D[li][0:DIN, P * oc:P * (oc + 1)],
                            w2D[li][DIN:DIN + 1, P * oc:P * (oc + 1)], NPAIR, P)
                        for ti in (1, 0):
                            tt = 2 * half + ti
                            bank = ps[4 + (oc * 2 + ti) % 2]
                            for kc in range(NPAIR):
                                w = (tuple(hb_done) + ((cnm2, cv2),
                                     ("s_dve", E.c("s_dve")),
                                     ("s_act", E.c("s_act")))) if kc == 0 else ()
                                E.emit("tensor", lambda e, b=bank, wv=wv2, kc=kc, ti=ti,
                                       hh=HB[kc]: e.matmul(
                                           out=b[:, :], lhsT=wv[:, kc, :],
                                           rhs=hh[:, 512 * ti:512 * (ti + 1)],
                                           start=(kc == 0), stop=False),
                                       waits=w)
                            E.emit("tensor", lambda e, b=bank, bv=bv2,
                                   o1=C["ones_r512"]: e.matmul(
                                       out=b[:, :], lhsT=bv, rhs=o1,
                                       start=False, stop=True),
                                   inc="s_pe")
                            E.emit("vector", lambda e, b=bank,
                                   o=x_f[oc][:, 1 + 512 * tt:1 + 512 * (tt + 1)]:
                                   e.tensor_tensor(out=o, in0=b[:, :], in1=o,
                                                   op=ALU.add),
                                   waits=(("s_pe", E.ct["s_pe"]),), inc="s_dve")
                        wl2.done(par2)

            for tt in range(4):
                for c in range(NCH):
                    E.dma("sync", xin_t, xh_in[s, c, :, 512 * tt:512 * (tt + 1)],
                          inc="d_in", waits=(("s_dve", E.c("s_dve")),))
                    E.emit("vector", lambda e, o=Af[c], i=xin_t,
                           x2=x_f[c][:, 1 + 512 * tt:1 + 512 * (tt + 1)]:
                           e.tensor_tensor(out=o[:, 0:512], in0=i, in1=x2, op=ALU.add),
                           waits=(("d_in", E.ct["d_in"]),), inc="s_dve")
                sum_t = [Af[c][:, 0:512] for c in range(NCH)]
                out_t = [Af[c][:, 512:1024] for c in range(NCH)]
                sr = (("s_dve", E.c("s_dve")), ("s_act", E.c("s_act")))
                emit_ln_tile(E, C, sum_t, out_t, sq_scr, ps[4][:, :], ps[5][:, :],
                             ps[6][:, :], sc, sum_ready=sr)
                lnv = E.c("s_dve")
                for k4 in range(4):
                    for c in range(NCH):
                        E.emit("tensor", lambda e, o=psb[:, 0:P], i=out_t[c], k4=k4,
                               idn=identb: e.transpose(
                                   out=o, in_=i[:, P * k4:P * (k4 + 1)], identity=idn),
                               waits=(("s_dve", lnv), ("s_act", E.c("s_act")))
                               if c == 0 and k4 == 0 else
                               (("s_act", E.c("s_act")),), inc="s_pe")
                        E.emit("scalar", lambda e, o=ost_f, c=c, i=psb[:, 0:P]: e.copy(
                            out=o[:, P * c:P * (c + 1)], in_=i),
                            waits=(("s_pe", E.ct["s_pe"]),), inc="s_act")
                    E.dma("gpsimd", y_out[s, 512 * tt + P * k4:512 * tt + P * (k4 + 1), :],
                          ost_f, inc="d_out", waits=(("s_act", E.ct["s_act"]),))
        E.wait("gpsimd", "d_out", E.c("d_out"))

        block = ctx.enter_context(nc.Block())
        E.flush(block)
    return nc



DIN = 2816
NPAIR = 22
WQ = 8.0


def host_constants_D():
    c = {}
    n = T
    half = np.arange(0, HD, 2, dtype=np.float64)
    inv_freq = 1.0 / 10000.0 ** (half / HD)
    t = np.arange(n, dtype=np.float64)
    freqs = t[:, None] * inv_freq[None, :]
    freqs = np.concatenate([freqs, freqs], -1)
    base_scale = (half + 0.4 * HD) / (1.4 * HD)
    power = (t - n // 2) / (256 // 2)
    scale = base_scale[None, :] ** power[:, None]
    scale = np.concatenate([scale, scale], -1)
    cos, sin = np.cos(freqs), np.sin(freqs)
    def fm(tab):
        return np.ascontiguousarray(
            np.tile(tab.T, (2, 1)).astype(BF))
    c["D_csq"] = fm(cos * scale)
    c["D_snq"] = fm(sin * scale)
    c["D_csk"] = fm(cos / scale)
    c["D_snk"] = fm(sin / scale)
    pm = np.zeros((P, P), np.float64)
    for blk in range(2):
        o = 64 * blk
        for d2 in range(32):
            pm[o + d2 + 32, o + d2] = -1.0
            pm[o + d2, o + d2 + 32] = 1.0
    c["D_rotperm"] = np.ascontiguousarray(pm.astype(BF))
    qsel = np.zeros((P, NCH * 16), np.float64)
    for ch in range(NCH):
        for p in range(P):
            qsel[p, 16 * ch + 2 * ch + p // 64] = 1.0
    c["D_qsel"] = np.ascontiguousarray(qsel.astype(BF))
    qbro = np.zeros((16, NCH * P), np.float64)
    for ch in range(NCH):
        for p in range(P):
            qbro[2 * ch + p // 64, P * ch + p] = 1.0
    c["D_qbro"] = np.ascontiguousarray(qbro.astype(BF))
    rep = np.zeros((HD, P), np.float64)
    for p in range(P):
        rep[p % HD, p] = 1.0
    c["D_rep64"] = np.ascontiguousarray(rep.astype(BF))
    q = np.arange(P)[:, None]
    k = np.arange(P)[None, :]
    c["D_mTadd"] = _f32(np.where(k.T > q.T * 0 + q.T - (q.T - q.T), 0.0, 0.0))
    c["D_mTadd"] = _f32(np.where(np.arange(P)[None, :] < np.arange(P)[:, None], NEG, 0.0))
    c["D_mT2add"] = _f32(np.where(np.arange(P)[None, :] > np.arange(P)[:, None], NEG, 0.0))
    c["D_mWadd"] = _f32(np.where(np.arange(P)[:, None] > np.arange(P)[None, :], NEG, 0.0))
    return c


def prep_weights_D(inp):
    w = {}
    w["D_wqkv"] = _f32(inp["L1_ffn_Wqkv"])
    w["D_wo"] = _f32(inp["L1_ffn_Wo"])
    W1 = _f32(inp["L1_ffn_W1"])
    b1 = _f32(inp["L1_ffn_b1"])
    w1p = np.zeros((6, D + 1, 2 * DIN), np.float32)
    for i in range(6):
        a = np.concatenate([W1[i][:, :2730], b1[i][None, :2730]], 0)
        g = np.concatenate([W1[i][:, 2730:], b1[i][None, 2730:]], 0)
        ap = np.zeros((D + 1, DIN), np.float32); ap[:, :2730] = a
        gp = np.zeros((D + 1, DIN), np.float32); gp[:, :2730] = g
        for op in range(NPAIR):
            w1p[i][:, 256 * op:256 * op + 128] = ap[:, 128 * op:128 * (op + 1)]
            w1p[i][:, 256 * op + 128:256 * (op + 1)] = gp[:, 128 * op:128 * (op + 1)]
    w["D_w1p"] = _f32(w1p)
    W2 = _f32(inp["L1_ffn_W2"])
    b2 = _f32(inp["L1_ffn_b2"])
    w2p = np.zeros((6, DIN + 1, D), np.float32)
    w2p[:, :2730, :] = W2
    w2p[:, DIN, :] = b2
    w["D_w2p"] = _f32(w2p)
    w["D_lng"] = _f32(_f32(inp["L1_ffn_ln_g"]).reshape(6, NCH, P)
                      .transpose(2, 0, 1).reshape(P, 48))
    w["D_lnb"] = _f32(_f32(inp["L1_ffn_ln_b"]).reshape(6, NCH, P)
                      .transpose(2, 0, 1).reshape(P, 48))
    w["D_qscaleT"] = _f32(_f32(inp["L1_ffn_q_scale"]).T)
    w["D_kscaleT"] = _f32(_f32(inp["L1_ffn_k_scale"]).T)
    return w


def build_D(race_detect=False, upto=99):
    nc = bass.Bass("TRN2", target_bir_lowering=False, detect_race_conditions=race_detect)
    xh_in = nc.dram_tensor("xh", [NSEQ, NCH, P, T], BF16, kind="ExternalInput").ap()
    wqkvD = nc.dram_tensor("D_wqkv", [6, D, 3 * D], F32, kind="ExternalInput").ap()
    woD = nc.dram_tensor("D_wo", [6, D, D], F32, kind="ExternalInput").ap()
    w1D = nc.dram_tensor("D_w1p", [6, D + 1, 2 * DIN], F32, kind="ExternalInput").ap()
    w2D = nc.dram_tensor("D_w2p", [6, DIN + 1, D], F32, kind="ExternalInput").ap()
    lngD = nc.dram_tensor("D_lng", [P, 48], F32, kind="ExternalInput").ap()
    lnbD = nc.dram_tensor("D_lnb", [P, 48], F32, kind="ExternalInput").ap()
    qscD = nc.dram_tensor("D_qscaleT", [HD, 6], F32, kind="ExternalInput").ap()
    kscD = nc.dram_tensor("D_kscaleT", [HD, 6], F32, kind="ExternalInput").ap()
    csqD = nc.dram_tensor("D_csq", [P, T], BF16, kind="ExternalInput").ap()
    snqD = nc.dram_tensor("D_snq", [P, T], BF16, kind="ExternalInput").ap()
    cskD = nc.dram_tensor("D_csk", [P, T], BF16, kind="ExternalInput").ap()
    snkD = nc.dram_tensor("D_snk", [P, T], BF16, kind="ExternalInput").ap()
    rotD = nc.dram_tensor("D_rotperm", [P, P], BF16, kind="ExternalInput").ap()
    qselD = nc.dram_tensor("D_qsel", [P, NCH * 16], BF16, kind="ExternalInput").ap()
    qbroD = nc.dram_tensor("D_qbro", [16, NCH * P], BF16, kind="ExternalInput").ap()
    rep64D = nc.dram_tensor("D_rep64", [HD, P], BF16, kind="ExternalInput").ap()
    mTaD = nc.dram_tensor("D_mTadd", [P, P], F32, kind="ExternalInput").ap()
    mT2aD = nc.dram_tensor("D_mT2add", [P, P], F32, kind="ExternalInput").ap()
    mWaD = nc.dram_tensor("D_mWadd", [P, P], F32, kind="ExternalInput").ap()
    mdiagD = nc.dram_tensor("mask_diag_add", [P, P], F32, kind="ExternalInput").ap()
    identD = nc.dram_tensor("ident_bf", [P, P], BF16, kind="ExternalInput").ap()
    identfD = nc.dram_tensor("ident_f32", [P, P], F32, kind="ExternalInput").ap()
    y_out = nc.dram_tensor("y", [NSEQ, T, D], F32, kind="ExternalOutput").ap()

    with ExitStack() as ctx:
        E = Em(nc, ctx)
        ab = Arena(nc, ctx, "ab", 64576, BF16)
        af = Arena(nc, ctx, "af", 15872, F32)
        C = {}
        C["ones_col"] = ab.alloc(64)[:, 0:1]
        C["ones_row1"] = ab.alloc(P)[0:1, 0:P]
        C["ones_r512"] = ab.alloc(512)[0:1, :]
        C["ones_1x64"] = ab.alloc(64)[0:1, :]
        for nm in ("ones_col", "ones_row1", "ones_r512", "ones_1x64"):
            E.emit("gpsimd", lambda e, v=C[nm]: e.memset(v, 1.0), inc="s_pool")
        C["ready_pool"] = E.ct["s_pool"]

        x_f = [ab.alloc(T + 64) for _ in range(NCH)]
        REG = ab.alloc(30720)
        Af = [REG[:, 2048 * c:2048 * (c + 1)] for c in range(NCH)]
        tbl_cs = REG[:, 16384:18432]
        tbl_sn = REG[:, 18432:20480]
        qaug = REG[0:HD + 1, 20480:22528]
        kaug = REG[0:HD + 1, 22528:24576]
        kaug_full = REG[:, 22528:24576]
        vtok = REG[:, 24576:26656]
        pT = REG[:, 26656:27168]
        rnB_sb = REG[:, 27168:27680]
        rn_bf = REG[0:16, 27680:29728]
        HB = [REG[:, 1024 * op:1024 * (op + 1)] for op in range(NPAIR)]
        xn = [REG[:, 22528 + 1024 * c:22528 + 1024 * (c + 1)] for c in range(NCH)]
        q_cur = ab.alloc(T)
        k_cur = ab.alloc(T)
        rotp = ab.alloc(P)
        qsel = ab.alloc(NCH * 16)
        qbro = ab.alloc(NCH * P)[0:16, :]
        rep64 = ab.alloc(P)[0:HD, :]
        identb = ab.alloc(P)
        recB_sb = ab.alloc(P)
        rec_b = ab.alloc(P)[0:1, :]
        mrow_bf = ab.alloc(P)[0:16, :]
        sq_t = ab.alloc(512)
        gl_t = ab.alloc(512)
        t2_t = ab.alloc(512)
        sq_scr = ab.alloc(512)
        sc = {"mean_b": ab.alloc(512)[0:1, :], "rstd_b": ab.alloc(512)[0:1, :]}
        xin_t = ab.alloc(512)
        qsc_bf = ab.alloc(64)[0:HD, 0:6]
        ksc_bf = ab.alloc(64)[0:HD, 0:6]

        lng = af.alloc(48)[:, 0:48]
        lnb = af.alloc(48)[:, 0:48]
        qs_col = af.alloc(64)[:, 0:1]
        ks_col = af.alloc(64)[:, 0:1]
        qsc_sb = af.alloc(64)[0:HD, 0:6]
        ksc_sb = af.alloc(64)[0:HD, 0:6]
        mq = af.alloc(16)[:, 0:16]
        rn_f = af.alloc(T)[0:16, :]
        rec_f = af.alloc(P)[0:1, :]
        mTa = af.alloc(P)
        mT2a = af.alloc(P)
        mWa = af.alloc(P)
        mdiag = af.alloc(P)
        identf = af.alloc(P)
        ost_f = af.alloc(1024)
        sc["mean2"] = af.alloc(512)[0:1, :]
        sc["var"] = af.alloc(512)[0:1, :]

        wl = WLoader(E, ab, af, 2816, 256, "w")
        wl2 = wl

        ps = [ctx.enter_context(nc.psum_tensor(f"ps{i}", [P, 512], F32)) for i in range(7)]
        psb = ctx.enter_context(nc.psum_tensor("psb", [P, 512], BF16))

        for (dst, srcD) in ((rotp, rotD), (qsel, qselD), (qbro, qbroD),
                            (rep64, rep64D), (identb, identD)):
            E.dma("sync", dst, srcD[:, :], inc="d_c")
        for (dst, srcD) in ((lng, lngD), (lnb, lnbD), (qsc_sb, qscD), (ksc_sb, kscD),
                            (mTa, mTaD), (mT2a, mT2aD), (mWa, mWaD), (mdiag, mdiagD),
                            (identf, identfD)):
            E.dma("sync", dst, srcD[:, :], inc="d_c")
        consts_ok = (("d_c", E.ct["d_c"]), ("s_pool", E.ct["s_pool"]))
        E.emit("gpsimd", lambda e, o=qsc_bf, i=qsc_sb: e.tensor_copy(out=o, in_=i),
               waits=consts_ok, inc="s_pool")
        E.emit("gpsimd", lambda e, o=ksc_bf, i=ksc_sb: e.tensor_copy(out=o, in_=i),
               inc="s_pool")
        consts_ok = (("d_c", E.ct["d_c"]), ("s_pool", E.ct["s_pool"]))

        for s in range(NSEQ):
            xw = [("s_dve", E.c("s_dve")), ("s_pe", E.c("s_pe")),
                  ("s_act", E.c("s_act")), ("d_out", E.c("d_out"))]
            for c in range(NCH):
                E.dma("sync", x_f[c][:, 1:1 + T], xh_in[s, c], inc="d_in",
                      waits=tuple(xw) if c == 0 else ())
            E.emit("gpsimd", lambda e: [e.memset(x_f[c][:, 0:1], 0.0)
                                        for c in range(NCH)][-1],
                   waits=(("d_in", E.ct["d_in"]),), inc="s_pool")
            x_ready = (("d_in", E.ct["d_in"]), ("s_pool", E.ct["s_pool"])) + consts_ok

            for li in range(6):
                shx = [x_f[c][:, 1:1 + T] if c < 4 else x_f[c][:, 0:T]
                       for c in range(NCH)]
                E.emit("tensor", lambda e, o=ps[6][:, 0:1], l=rep64, r=qsc_bf, li=li:
                       e.matmul(out=o, lhsT=l, rhs=r[:, li:li + 1], start=True, stop=True),
                       waits=tuple(x_ready) + (("s_act", E.c("s_act")),
                                               ("s_dve", E.c("s_dve"))), inc="s_pe")
                E.emit("scalar", lambda e, o=qs_col, i=ps[6][:, 0:1]: e.copy(out=o, in_=i),
                       waits=(("s_pe", E.ct["s_pe"]),), inc="s_act")
                E.emit("tensor", lambda e, o=ps[6][:, 0:1], l=rep64, r=ksc_bf, li=li:
                       e.matmul(out=o, lhsT=l, rhs=r[:, li:li + 1], start=True, stop=True),
                       waits=(("s_act", E.ct["s_act"]),), inc="s_pe")
                E.emit("scalar", lambda e, o=ks_col, i=ps[6][:, 0:1]: e.copy(out=o, in_=i),
                       waits=(("s_pe", E.ct["s_pe"]),), inc="s_act")
                qsks_ok = (("s_act", E.ct["s_act"]),)

                for h in range(NH):
                    ch, rb = h // 2, HD * (h % 2)
                    if h % 2 == 0:
                        for (which, dst, csD_, snD_, s_col) in (
                                ("q", q_cur, csqD, snqD, qs_col),
                                ("k", k_cur, cskD, snkD, ks_col)):
                            coff = 0 if which == "q" else D
                            cs_t, sn_t = tbl_cs, tbl_sn
                            E.dma("sync", tbl_cs, csD_[:, :], inc="d_tb",
                                  waits=(("s_dve", E.c("s_dve")),))
                            E.dma("sync", tbl_sn, snD_[:, :], inc="d_tb")
                            tbl_ok = E.ct["d_tb"]
                            norm_ps = ps[5][0:16, :]
                            def pdst(oc, tt, bank, pev, dst=dst, ch=ch, np_=norm_ps):
                                E.emit("scalar", lambda e, b=bank, o=dst, tt=tt: e.copy(
                                    out=o[:, 512 * tt:512 * (tt + 1)], in_=b[:, :]),
                                    waits=(("s_pe", pev), ("s_dve", E.c("s_dve"))),
                                    inc="s_act")
                                av = E.ct["s_act"]
                                E.emit("vector", lambda e, o=sq_t, i=dst, tt=tt:
                                       e.tensor_tensor(
                                           out=o, in0=i[:, 512 * tt:512 * (tt + 1)],
                                           in1=i[:, 512 * tt:512 * (tt + 1)],
                                           op=ALU.mult),
                                       waits=(("s_act", av),), inc="s_dve")
                                E.emit("tensor", lambda e, o=np_, l=qsel, ch=ch, tt=tt:
                                       e.matmul(out=o, lhsT=l[:, 16 * ch:16 * (ch + 1)],
                                                rhs=sq_t, start=True, stop=True),
                                       waits=(("s_dve", E.ct["s_dve"]),), inc="s_pe")
                                E.emit("scalar", lambda e, o=rn_f, i=np_, tt=tt:
                                       e.activation(out=o[0:16, 512 * tt:512 * (tt + 1)],
                                                    in_=i, func=AF.Sqrt),
                                       waits=(("s_pe", E.ct["s_pe"]),), inc="s_act")
                                E.emit("vector", lambda e, o=rn_f, tt=tt:
                                       e.tensor_scalar_max(
                                           out=o[0:16, 512 * tt:512 * (tt + 1)],
                                           in0=o[0:16, 512 * tt:512 * (tt + 1)],
                                           scalar1=1e-12),
                                       waits=(("s_act", E.ct["s_act"]),), inc="s_dve")
                                E.emit("vector", lambda e, o=rn_f, tt=tt: e.reciprocal(
                                    out=o[0:16, 512 * tt:512 * (tt + 1)],
                                    in_=o[0:16, 512 * tt:512 * (tt + 1)]),
                                    waits=(("s_dve", E.ct["s_dve"]),), inc="s_dve")
                                E.emit("vector", lambda e, o=rn_bf, i=rn_f, tt=tt:
                                       e.tensor_copy(
                                           out=o[0:16, 512 * tt:512 * (tt + 1)],
                                           in_=i[0:16, 512 * tt:512 * (tt + 1)]),
                                       waits=(("s_dve", E.ct["s_dve"]),), inc="s_dve")
                                return ("s_dve", E.ct["s_dve"])
                            emit_proj_fmajor(
                                E, C, wl, wqkvD[li][:, coff:coff + D], [ch], shx,
                                None, ps, [0, 1, 2, 3],
                                tuple(x_ready) + qsks_ok +
                                (("s_dve", E.c("s_dve")), ("s_act", E.c("s_act"))),
                                dst_fn=pdst, has_bias=False)
                            for tt in range(4):
                                sl = slice(512 * tt, 512 * (tt + 1))
                                E.emit("tensor", lambda e, o=ps[4][:, :], l=qbro, ch=ch,
                                       r=rn_bf, sl=sl: e.matmul(
                                           out=o, lhsT=l[:, P * ch:P * (ch + 1)],
                                           rhs=r[0:16, sl], start=True, stop=True),
                                       waits=(("s_dve", E.c("s_dve")),
                                              ("s_act", E.c("s_act"))), inc="s_pe")
                                E.emit("scalar", lambda e, o=rnB_sb, i=ps[4][:, :]:
                                       e.copy(out=o, in_=i),
                                       waits=(("s_pe", E.ct["s_pe"]),), inc="s_act")
                                E.emit("vector", lambda e, o=dst, sl=sl, sc2=s_col,
                                       rb2=rnB_sb: e.scalar_tensor_tensor(
                                           out=o[:, sl], in0=o[:, sl], scalar=sc2,
                                           in1=rb2, op0=ALU.mult, op1=ALU.mult),
                                       waits=(("s_act", E.ct["s_act"]),), inc="s_dve")
                                q1v = E.ct["s_dve"]
                                E.emit("tensor", lambda e, o=ps[4][:, :], l=rotp, i=dst,
                                       sl=sl: e.matmul(out=o, lhsT=l, rhs=i[:, sl],
                                                       start=True, stop=True),
                                       waits=(("s_dve", q1v), ("s_act", E.c("s_act"))),
                                       inc="s_pe")
                                E.emit("vector", lambda e, o=t2_t, i=ps[4][:, :],
                                       sn_t=sn_t, sl=sl: e.tensor_tensor(
                                           out=o, in0=i, in1=sn_t[:, sl], op=ALU.mult),
                                       waits=(("s_pe", E.ct["s_pe"]), ("d_tb", tbl_ok)),
                                       inc="s_dve")
                                E.emit("vector", lambda e, o=dst, sl=sl, cs_t=cs_t:
                                       e.tensor_tensor(out=o[:, sl], in0=o[:, sl],
                                                       in1=cs_t[:, sl], op=ALU.mult),
                                       waits=(("s_dve", E.ct["s_dve"]),), inc="s_dve")
                                E.emit("vector", lambda e, o=dst, sl=sl, t2=t2_t:
                                       e.tensor_tensor(out=o[:, sl], in0=o[:, sl],
                                                       in1=t2, op=ALU.add),
                                       waits=(("s_dve", E.ct["s_dve"]),), inc="s_dve")
                        E.emit("gpsimd", lambda e, v=vtok: e.memset(
                            v.rearrange("p (a h c) -> p a h c", a=16, h=2)
                            [:, :, :, HD:HD + 1], 1.0),
                            waits=(("s_pe", E.c("s_pe")),), inc="s_pool")
                        vone_ok = E.ct["s_pool"]
                        coff = 2 * D + P * ch
                        wv, _, cnm, cv, par = wl.load(wqkvD[li][:, coff:coff + P],
                                                      None, NCH, P)
                        for tc in range(16):
                            bank = ps[4 + tc % 2]
                            for dc in range(NCH):
                                w = ((cnm, cv), ("s_dve", E.c("s_dve")),
                                     ("s_act", E.c("s_act"))) if dc == 0 else ()
                                E.emit("tensor", lambda e, b=bank, wv=wv, dc=dc, tc=tc,
                                       xs=shx[dc]: e.matmul(
                                           out=b[:, 0:P], lhsT=xs[:, P * tc:P * (tc + 1)],
                                           rhs=wv[:, dc, :], start=(dc == 0),
                                           stop=(dc == NCH - 1)),
                                       waits=w, inc="s_pe" if dc == NCH - 1 else None)
                            E.emit("vector", lambda e, b=bank, tc=tc, v=vtok:
                                   e.tensor_copy(
                                       out=v.rearrange("p (a h c) -> p a h c", a=16, h=2)
                                           [:, tc, :, 0:HD],
                                       in_=b[:, 0:P].rearrange("p (h c) -> p h c", h=2)),
                                   waits=(("s_pe", E.ct["s_pe"]),
                                          ("s_pool", vone_ok)), inc="s_dve")
                        wl.done(par)
                    qk_done = (("s_dve", E.c("s_dve")),)
                    qrows = q_cur[rb:rb + HD, :]
                    krows = k_cur[rb:rb + HD, :]

                    p1_red = []
                    for j in range(16):
                        ks = P * max(0, j - 2)
                        wdt = P * (j + 1) - ks
                        bank = ps[j % 2]
                        bw = [p1_red[j - 2]] if j >= 2 else                             [("s_act", E.c("s_act")), ("s_dve", E.c("s_dve"))]
                        E.emit("tensor", lambda e, b=bank, j=j, ks=ks, wdt=wdt,
                               qr=qrows, kr=krows: e.matmul(
                                   out=b[:, 0:wdt], lhsT=qr[:, P * j:P * (j + 1)],
                                   rhs=kr[:, ks:ks + wdt], start=True, stop=True),
                               waits=tuple(list(qk_done) + bw), inc="s_pe")
                        mmv = E.ct["s_pe"]
                        if wdt == 384:
                            E.emit("vector", lambda e, b=bank, m=mWa: e.tensor_tensor(
                                out=b[:, 0:P], in0=b[:, 0:P], in1=m, op=ALU.add),
                                waits=(("s_pe", mmv),), inc="s_dve")
                        E.emit("vector", lambda e, b=bank, wdt=wdt, m=mdiag:
                               e.tensor_tensor(out=b[:, wdt - P:wdt],
                                               in0=b[:, wdt - P:wdt], in1=m, op=ALU.add),
                               waits=(("s_pe", mmv), ("s_dve", E.c("s_dve"))),
                               inc="s_dve")
                        E.emit("vector", lambda e, b=bank, j=j, wdt=wdt, m=mq:
                               e.reduce_max(out=m[:, j:j + 1], in_=b[:, 0:wdt],
                                            axis=AX.X),
                               waits=(("s_dve", E.ct["s_dve"]),), inc="s_dve")
                        p1_red.append(("s_dve", E.ct["s_dve"]))
                    E.emit("tensor", lambda e, m=mq, idn=identf, o=ps[6][0:16, 0:P]:
                           e.transpose(out=o, in_=m, identity=idn),
                           waits=(("s_dve", E.c("s_dve")),), inc="s_pe")
                    E.emit("scalar", lambda e, o=mrow_bf, i=ps[6][0:16, 0:P]: e.mul(
                        out=o, in_=i, mul=-1.0),
                        waits=(("s_pe", E.ct["s_pe"]),), inc="s_act")
                    E.dma("gpsimd", qaug[HD:HD + 1, :].rearrange("p (a b) -> p a b", a=16),
                          mrow_bf, inc="d_m",
                          waits=(("s_act", E.ct["s_act"]), ("s_pe", E.c("s_pe"))))
                    E.emit("vector", lambda e, o=qaug[0:HD, :], i=qrows: e.tensor_copy(
                        out=o, in_=i),
                        waits=(("s_pe", E.c("s_pe")),), inc="s_dve")
                    E.emit("vector", lambda e, o=kaug[0:HD, :], i=krows: e.tensor_copy(
                        out=o, in_=i), inc="s_dve")
                    if h == 0:
                        E.emit("gpsimd", lambda e, o=kaug[HD:HD + 1, :]: e.memset(o, 1.0),
                               waits=(("s_pe", E.c("s_pe")),), inc="s_pool")
                    aug_done = (("s_dve", E.ct["s_dve"]), ("d_m", E.c("d_m")),
                                ("s_pool", E.c("s_pool")))

                    exp_prev = [("s_act", E.c("s_act"))] * 2
                    for i in range(16):
                        q0 = P * i
                        qe = min(T, P * (i + 3))
                        wdt = qe - q0
                        bank = ps[i % 2]
                        E.emit("tensor", lambda e, b=bank, i=i, q0=q0, wdt=wdt,
                               ka=kaug, qa=qaug: e.matmul(
                                   out=b[:, 0:wdt], lhsT=ka[:, P * i:P * (i + 1)],
                                   rhs=qa[:, q0:q0 + wdt], start=True, stop=True),
                               waits=tuple(list(aug_done) + [exp_prev[i % 2]]),
                               inc="s_pe")
                        mmv = E.ct["s_pe"]
                        E.emit("vector", lambda e, b=bank, m=mTa: e.tensor_tensor(
                            out=b[:, 0:P], in0=b[:, 0:P], in1=m, op=ALU.add),
                            waits=(("s_pe", mmv), ("s_dve", E.c("s_dve"))), inc="s_dve")
                        if wdt == 384:
                            E.emit("vector", lambda e, b=bank, m=mT2a: e.tensor_tensor(
                                out=b[:, 256:384], in0=b[:, 256:384], in1=m, op=ALU.add),
                                waits=(("s_dve", E.ct["s_dve"]),), inc="s_dve")
                        E.emit("scalar", lambda e, b=bank, wdt=wdt, p=pT: e.activation(
                            out=p[:, 0:wdt], in_=b[:, 0:wdt], func=AF.Exp, scale=WQ),
                            waits=(("s_dve", E.ct["s_dve"]),), inc="s_act")
                        exp_prev[i % 2] = ("s_act", E.ct["s_act"])
                        exp_done = E.ct["s_act"]
                        for j in range(i, min(i + 3, 16)):
                            qoff = P * (j - i)
                            if P * j >= qe:
                                continue
                            ob = ps[2 + j % 3]
                            E.emit("tensor", lambda e, ob=ob, i=i, j=j, qoff=qoff, h=h,
                                   v=vtok, p=pT: e.matmul(
                                       out=ob[0:HD + 1, 0:P],
                                       lhsT=v[:, 130 * i + 65 * (h % 2):
                                              130 * i + 65 * (h % 2) + 65],
                                       rhs=p[:, qoff:qoff + P],
                                       start=(i == max(0, j - 2)), stop=(i == j)),
                                   waits=(("s_act", exp_done),) if j == i else (),
                                   inc="s_pe" if i == j else None)
                            if i == j:
                                pvv = E.ct["s_pe"]
                                E.emit("vector", lambda e, ob=ob, o=rec_f: e.tensor_copy(
                                    out=o[0:1, 0:P], in_=ob[HD:HD + 1, 0:P]),
                                    waits=(("s_pe", pvv), ("s_act", E.c("s_act"))),
                                    inc="s_dve")
                                E.emit("vector", lambda e, o=rec_f: e.reciprocal(
                                    out=o[0:1, 0:P], in_=o[0:1, 0:P]),
                                    waits=(("s_dve", E.ct["s_dve"]),), inc="s_dve")
                                E.emit("vector", lambda e, o=rec_b, i2=rec_f:
                                       e.tensor_copy(out=o, in_=i2[0:1, 0:P]),
                                       waits=(("s_dve", E.ct["s_dve"]),), inc="s_dve")
                                E.emit("tensor", lambda e, o=ps[6][0:HD, 0:P], r=rec_b,
                                       l=C["ones_1x64"]: e.matmul(
                                           out=o, lhsT=l, rhs=r, start=True, stop=True),
                                       waits=(("s_dve", E.ct["s_dve"]),
                                              ("s_act", E.c("s_act"))), inc="s_pe")
                                E.emit("scalar", lambda e, o=recB_sb[0:HD, 0:P],
                                       i2=ps[6][0:HD, 0:P]: e.copy(out=o, in_=i2),
                                       waits=(("s_pe", E.ct["s_pe"]),
                                              ("s_dve", E.c("s_dve"))), inc="s_act")
                                E.emit("vector", lambda e, ob=ob, j=j, ch=ch, rb=rb,
                                       o=Af[ch], rB=recB_sb: e.tensor_tensor(
                                           out=o[rb:rb + HD, P * j:P * (j + 1)],
                                           in0=ob[0:HD, 0:P], in1=rB[0:HD, 0:P],
                                           op=ALU.mult),
                                       waits=(("s_act", E.ct["s_act"]),), inc="s_dve")
                attn_done = (("s_dve", E.c("s_dve")), ("s_act", E.c("s_act")))

                def wodst(oc, tt, bank, pev):
                    return ("s_dve", E.emit(
                        "vector", lambda e, b=bank, o=x_f[oc][:, 1 + 512 * tt:
                                                              1 + 512 * (tt + 1)]:
                        e.tensor_tensor(out=o, in0=b[:, :], in1=o, op=ALU.add),
                        waits=(("s_pe", pev),), inc="s_dve"))
                emit_proj_fmajor(E, C, wl, woD[li], list(range(NCH)), Af, None, ps,
                                 [0, 1, 2, 3], attn_done, dst_fn=wodst, has_bias=False)

                gc = [lng[:, 8 * li + c:8 * li + c + 1] for c in range(NCH)]
                bc = [lnb[:, 8 * li + c:8 * li + c + 1] for c in range(NCH)]
                for half in (1, 0):
                    tts = [2 * half + 1, 2 * half]
                    for tt in tts:
                        sl = slice(512 * tt, 512 * (tt + 1))
                        shs = [x_f[c][:, 1 + 512 * tt:1 + 512 * (tt + 1)] if c < 4
                               else x_f[c][:, 512 * tt:512 * (tt + 1)]
                               for c in range(NCH)]
                        ho = 512 * (tt - 2 * half)
                        out_t = [xn[c][:, ho:ho + 512] for c in range(NCH)]
                        sr = (("s_dve", E.c("s_dve")), ("s_act", E.c("s_act")))
                        emit_ln_tile(E, C, shs, out_t, sq_scr, ps[4][:, :], ps[5][:, :],
                                     ps[6][:, :], sc, sum_ready=sr, gains=(gc, bc))
                    xn_done = (("s_dve", E.c("s_dve")),)
                    for op in range(NPAIR):
                        wv, bv, cnm, cv, par = wl.load(
                            w1D[li][0:D, 256 * op:256 * (op + 1)],
                            w1D[li][D:D + 1, 256 * op:256 * (op + 1)], NCH, 256)
                        for ti in range(2):
                            banka, bankg = ps[2 * (ti % 2)], ps[2 * (ti % 2) + 1]
                            for (bank, part) in ((banka, 0), (bankg, 1)):
                                for dc in range(NCH):
                                    w = (tuple(xn_done) + ((cnm, cv),
                                         ("s_dve", E.c("s_dve")),
                                         ("s_act", E.c("s_act")))) if dc == 0 else ()
                                    E.emit("tensor", lambda e, b=bank, wv=wv, dc=dc,
                                           ti=ti, part=part, x2=xn[dc]: e.matmul(
                                               out=b[:, :],
                                               lhsT=wv[:, dc, P * part:P * (part + 1)],
                                               rhs=x2[:, 512 * ti:512 * (ti + 1)],
                                               start=(dc == 0), stop=False),
                                           waits=w)
                                E.emit("tensor", lambda e, b=bank, bv=bv, part=part,
                                       o1=C["ones_r512"]: e.matmul(
                                           out=b[:, :],
                                           lhsT=bv[0:1, P * part:P * (part + 1)],
                                           rhs=o1, start=False, stop=True),
                                       inc="s_pe")
                            guv = E.ct["s_pe"]
                            E.emit("scalar", lambda e, o=gl_t, b=bankg: e.activation(
                                out=o, in_=b[:, :], func=AF.Gelu),
                                waits=(("s_pe", guv),), inc="s_act")
                            E.emit("vector", lambda e, o=HB[op], ti=ti, a=banka, g=gl_t:
                                   e.tensor_tensor(out=o[:, 512 * ti:512 * (ti + 1)],
                                                   in0=a[:, :], in1=g, op=ALU.mult),
                                   waits=(("s_pe", guv), ("s_act", E.ct["s_act"]),
                                          ("s_dve", E.c("s_dve"))), inc="s_dve")
                        wl.done(par)
                    hb_done = (("s_dve", E.c("s_dve")),)
                    for oc in range(NCH):
                        wv2, bv2, cnm2, cv2, par2 = wl2.load(
                            w2D[li][0:DIN, P * oc:P * (oc + 1)],
                            w2D[li][DIN:DIN + 1, P * oc:P * (oc + 1)], NPAIR, P)
                        for ti in (1, 0):
                            tt = 2 * half + ti
                            bank = ps[4 + (oc * 2 + ti) % 2]
                            for kc in range(NPAIR):
                                w = (tuple(hb_done) + ((cnm2, cv2),
                                     ("s_dve", E.c("s_dve")),
                                     ("s_act", E.c("s_act")))) if kc == 0 else ()
                                E.emit("tensor", lambda e, b=bank, wv=wv2, kc=kc, ti=ti,
                                       hh=HB[kc]: e.matmul(
                                           out=b[:, :], lhsT=wv[:, kc, :],
                                           rhs=hh[:, 512 * ti:512 * (ti + 1)],
                                           start=(kc == 0), stop=False),
                                       waits=w)
                            E.emit("tensor", lambda e, b=bank, bv=bv2,
                                   o1=C["ones_r512"]: e.matmul(
                                       out=b[:, :], lhsT=bv, rhs=o1,
                                       start=False, stop=True),
                                   inc="s_pe")
                            E.emit("vector", lambda e, b=bank,
                                   o=x_f[oc][:, 1 + 512 * tt:1 + 512 * (tt + 1)]:
                                   e.tensor_tensor(out=o, in0=b[:, :], in1=o,
                                                   op=ALU.add),
                                   waits=(("s_pe", E.ct["s_pe"]),), inc="s_dve")
                        wl2.done(par2)

            for tt in range(4):
                for c in range(NCH):
                    E.dma("sync", xin_t, xh_in[s, c, :, 512 * tt:512 * (tt + 1)],
                          inc="d_in", waits=(("s_dve", E.c("s_dve")),))
                    E.emit("vector", lambda e, o=Af[c], i=xin_t,
                           x2=x_f[c][:, 1 + 512 * tt:1 + 512 * (tt + 1)]:
                           e.tensor_tensor(out=o[:, 0:512], in0=i, in1=x2, op=ALU.add),
                           waits=(("d_in", E.ct["d_in"]),), inc="s_dve")
                sum_t = [Af[c][:, 0:512] for c in range(NCH)]
                out_t = [Af[c][:, 512:1024] for c in range(NCH)]
                sr = (("s_dve", E.c("s_dve")), ("s_act", E.c("s_act")))
                emit_ln_tile(E, C, sum_t, out_t, sq_scr, ps[4][:, :], ps[5][:, :],
                             ps[6][:, :], sc, sum_ready=sr)
                lnv = E.c("s_dve")
                for k4 in range(4):
                    for c in range(NCH):
                        E.emit("tensor", lambda e, o=psb[:, 0:P], i=out_t[c], k4=k4,
                               idn=identb: e.transpose(
                                   out=o, in_=i[:, P * k4:P * (k4 + 1)], identity=idn),
                               waits=(("s_dve", lnv), ("s_act", E.c("s_act")))
                               if c == 0 and k4 == 0 else
                               (("s_act", E.c("s_act")),), inc="s_pe")
                        E.emit("scalar", lambda e, o=ost_f, c=c, i=psb[:, 0:P]: e.copy(
                            out=o[:, P * c:P * (c + 1)], in_=i),
                            waits=(("s_pe", E.ct["s_pe"]), ("d_out", E.c("d_out"))),
                            inc="s_act")
                    E.dma("gpsimd", y_out[s, 512 * tt + P * k4:512 * tt + P * (k4 + 1), :],
                          ost_f, inc="d_out", waits=(("s_act", E.ct["s_act"]),))
        E.wait("gpsimd", "d_out", E.c("d_out"))

        block = ctx.enter_context(nc.Block())
        E.flush(block)
    return nc




def kernel(**inputs):
    consts = host_constants()
    cD = host_constants_D()
    wA = prep_weights_A(inputs)
    wB = prep_weights_B(inputs)
    wC = prep_weights_C(inputs)
    wD = prep_weights_D(inputs)
    x = np.ascontiguousarray(inputs["L0_attn_inputs"], dtype=np.float32)
    identb = np.eye(P, dtype=BF)

    ncA = build_A()
    in_maps = []
    for core in range(8):
        m = {"x0": x[2 * core:2 * core + 2]}
        m.update(wA)
        m.update({k: consts[k] for k in ("ident_f32", "mask_diag_add", "maskT_keep")})
        in_maps.append(m)
    res = run_bass_kernel_spmd(ncA, in_maps, core_ids=list(range(8)))
    xh = [np.asarray(r["xh"]) for r in res.results]

    ncB = build_B()
    in_maps = []
    for core in range(8):
        m = {"xh": xh[core], "ident_bf": identb}
        m.update(wB)
        in_maps.append(m)
    res = run_bass_kernel_spmd(ncB, in_maps, core_ids=list(range(8)))
    xh2 = [np.asarray(r["xh2"]) for r in res.results]

    ncC = build_C()
    in_maps = []
    for core in range(8):
        m = {"xh": xh2[core], "ident_bf": identb}
        m.update(wC)
        in_maps.append(m)
    res = run_bass_kernel_spmd(ncC, in_maps, core_ids=list(range(8)))
    xh3 = [np.asarray(r["xh3"]) for r in res.results]

    ncD = build_D()
    in_maps = []
    for core in range(8):
        m = {"xh": xh3[core], "ident_bf": identb,
             "ident_f32": consts["ident_f32"],
             "mask_diag_add": consts["mask_diag_add"]}
        m.update(cD)
        m.update(wD)
        in_maps.append(m)
    res = run_bass_kernel_spmd(ncD, in_maps, core_ids=list(range(8)))
    y = np.stack([np.asarray(r["y"]) for r in res.results])
    return np.ascontiguousarray(y.reshape(16, T, D))


if __name__ == "__main__":
    build_A()
    print("build_A ok")
